# version 24 (final)
# baseline (speedup 1.0000x reference)
import numpy as np
import ml_dtypes
from contextlib import ExitStack

import concourse.bass as bass
import concourse.mybir as mybir
from concourse.bass_utils import run_bass_kernel_spmd

F32 = mybir.dt.float32
BF16 = mybir.dt.bfloat16
AF = mybir.ActivationFunctionType
ALU = mybir.AluOpType
AX = mybir.AxisListType

NCORES = 8
S = 8192
DM = 2048
NDC = DM // 128
TOWN = 1024
HALO = 512
TLOC = TOWN + HALO
NH = 16
NKV = 4
WIN = 512
CAP = 20.0
HFF = 8192
GS = 64
INV_CTHR = float(1.0 / (0.5 + 2.0 ** -10))
TT = 512
EPS_LN = 1e-5
EPS_RMS = float(np.finfo(np.float32).eps)
SQ_SCALE = float(1.0 / (CAP * np.sqrt(128.0)))

BF = ml_dtypes.bfloat16


def _host_consts(core):
    start = core * TOWN - HALO
    pos = np.clip(np.arange(start, start + TLOC), 0, None).astype(np.float64)
    inv = 1.0 / (10000.0 ** (np.arange(0, 128, 2, dtype=np.float64) / 128.0))
    ang = pos[:, None] * inv[None, :]
    cos = np.cos(ang).astype(np.float32)
    sin = np.sin(ang).astype(np.float32)

    masks = np.zeros((2, 8, 128, 512), dtype=BF)
    j = np.arange(1024)[None, :]
    for bq in range(2):
        gblk = 2 * core + bq
        alpha = np.arange(512)[:, None]
        m = (j >= alpha) & (j <= alpha + WIN)
        if gblk == 0:
            m = m & (j >= WIN)
        masks[bq] = np.ascontiguousarray(m.T).astype(BF).reshape(8, 128, 512)

    gsel = np.zeros((16, 128, 32), dtype=BF)
    bsel = np.zeros((16, 32, 128), dtype=BF)
    for i in range(16):
        for p in range(128):
            g = ((128 * i + p) // GS) % 32
            gsel[i, p, g] = 1
            bsel[i, g, p] = 1
    return dict(rope_cos=cos, rope_sin=sin, masks=masks, gsel=gsel, bsel=bsel,
                idn=np.eye(128, dtype=BF), idnf=np.eye(128, dtype=np.float32),
                onesb=np.ones((128, 1), dtype=BF),
                onesf=np.ones((1, 128), dtype=np.float32))


class Sem:
    def __init__(self, stack, nc, name):
        self.h = stack.enter_context(nc.semaphore(name))
        self.n = 0

    def inc(self, k=1):
        self.n += k
        return self.n


class B:

    def __init__(self, nc, stack, blk):
        self.nc = nc
        self.stack = stack
        self.blk = blk
        self.dsems = {}
        self.tE = Sem(stack, nc, "tE")
        self.vE = Sem(stack, nc, "vE")
        self.sE = Sem(stack, nc, "sE")
        self.gE = Sem(stack, nc, "gE")

    def ds(self, name):
        if name not in self.dsems:
            self.dsems[name] = Sem(self.stack, self.nc, "d_" + name)
        return self.dsems[name]

    def bar(self):
        bb = dict(t=self.tE.n, v=self.vE.n, s=self.sE.n, g=self.gE.n)
        bb["dma"] = [(s.h, s.n) for s in self.dsems.values()]
        return bb

    def edge(self, eng, ins, which):
        sem = getattr(self, which)
        ins.then_inc(sem.h, 1)
        v = sem.inc()
        eng.wait_ge(sem.h, v)
        return v

    def wait_bar(self, eng, bar):
        eng.wait_ge(self.tE.h, bar["t"])
        eng.wait_ge(self.vE.h, bar["v"])
        eng.wait_ge(self.sE.h, bar["s"])
        eng.wait_ge(self.gE.h, bar["g"])
        for h, n in bar["dma"]:
            if n > 0:
                eng.wait_ge(h, n)



def emit_ln_stages(b, C, xsrc, xn_dst, T, first_bar):
    blk = b.blk
    xbf, ubuf = C["ln_xbf"], C["ln_ubuf"]
    msb, m2, e2, rssb = C["ln_msb"], C["ln_m2"], C["ln_e2"], C["ln_rssb"]
    ps_sx, ps_sxx, ps_M, ps_RS = C["ln_ps_sx"], C["ln_ps_sxx"], C["ln_ps_M"], C["ln_ps_RS"]
    xbfv = xbf.rearrange("p (k t) -> p k t", t=T)
    out = {}

    def s0():
        @blk.vector
        def _(v):
            b.wait_bar(v, first_bar)
            for i in range(NDC):
                li = v.tensor_copy(xbfv[:, i, :], xsrc(i))
            b.edge(v, li, "vE")
            for i in range(NDC):
                li = v.tensor_mul(xbfv[:, NDC + i, :], xbfv[:, i, :], xbfv[:, i, :])
            li.then_inc(b.vE.h, 1)
        out["v_sq"] = b.vE.inc()

    def s1():
        @blk.tensor
        def _(t, v_sq=out["v_sq"]):
            t.wait_ge(b.vE.h, v_sq)
            for i in range(NDC):
                _li = t.matmul(ps_sx[0:1, 0:T], C["onesb"][:, :], xbfv[:, i, :],
                         start=(i == 0), stop=(i == NDC - 1))
            for i in range(NDC):
                _li = t.matmul(ps_sxx[0:1, 0:T], C["onesb"][:, :], xbfv[:, NDC + i, :],
                         start=(i == 0), stop=(i == NDC - 1))
            _li.then_inc(b.tE.h, 1)
        out["t_stats"] = b.tE.inc()

    def s2():
        @blk.vector
        def _(v, t_stats=out["t_stats"]):
            v.wait_ge(b.tE.h, t_stats)
            li = v.tensor_scalar(msb[0:1, 0:T], ps_sx[0:1, 0:T], 1.0 / DM, None, op0=ALU.mult)
            b.edge(v, li, "vE")
            li = v.tensor_mul(m2[0:1, 0:T], msb[0:1, 0:T], msb[0:1, 0:T])
            b.edge(v, li, "vE")
            li = v.tensor_scalar(e2[0:1, 0:T], ps_sxx[0:1, 0:T], 1.0 / DM, EPS_LN,
                                 op0=ALU.mult, op1=ALU.add)
            b.edge(v, li, "vE")
            v.tensor_sub(e2[0:1, 0:T], e2[0:1, 0:T], m2[0:1, 0:T]).then_inc(b.vE.h, 1)
        v_var = b.vE.inc()

        @blk.scalar
        def _(s, v_var=v_var):
            s.wait_ge(b.vE.h, v_var)
            li = s.activation(e2[0:1, 0:T], e2[0:1, 0:T], AF.Ln)
            b.edge(s, li, "sE")
            s.activation(rssb[0:1, 0:T], e2[0:1, 0:T], AF.Exp, scale=-0.5).then_inc(b.sE.h, 1)
        out["s_rs"] = b.sE.inc()

    def s3():
        @blk.tensor
        def _(t, s_rs=out["s_rs"]):
            t.wait_ge(b.sE.h, s_rs)
            _li = t.matmul(ps_M[:, 0:T], C["onesf"][0:1, :], msb[0:1, 0:T], start=True, stop=True)
            _li = t.matmul(ps_RS[:, 0:T], C["onesf"][0:1, :], rssb[0:1, 0:T], start=True, stop=True)
            _li.then_inc(b.tE.h, 1)
        out["t_bc"] = b.tE.inc()

    def s4():
        @blk.vector
        def _(v, t_bc=out["t_bc"]):
            v.wait_ge(b.tE.h, t_bc)
            for i in range(NDC):
                li = v.scalar_tensor_tensor(ubuf[:, 0:T], xsrc(i), 1.0, ps_M[:, 0:T],
                                            op0=ALU.mult, op1=ALU.subtract)
                b.edge(v, li, "vE")
                li = v.tensor_mul(xn_dst(i), ubuf[:, 0:T], ps_RS[:, 0:T])
                if i < NDC - 1:
                    b.edge(v, li, "vE")
            li.then_inc(b.vE.h, 1)
        out["v_ln"] = b.vE.inc()

    return [s0, s1, s2, s3, s4], out


def emit_ln(b, C, xsrc, xn_dst, T, first_bar):
    stages, out = emit_ln_stages(b, C, xsrc, xn_dst, T, first_bar)
    for st in stages:
        st()
    return out["v_ln"]



def emit_quant(b, d, C):
    blk = b.blk
    nc = b.nc
    with ExitStack() as qs:
        e = qs.enter_context
        nat = e(nc.sbuf_tensor("q_nat", [128, 4 * 2048], F32))
        natb = e(nc.sbuf_tensor("q_natb", [128, 2 * 4 * 2048], BF16))
        aT = e(nc.sbuf_tensor("q_aT", [128, 2 * 16 * 512], BF16))
        sig = e(nc.sbuf_tensor("q_sig", [128, 2 * 16 * 512], BF16))
        sg = e(nc.sbuf_tensor("q_sg", [32, 2 * 512], BF16))
        thf = e(nc.sbuf_tensor("q_thf", [32, 2 * 512], F32))
        thb = e(nc.sbuf_tensor("q_thb", [32, 2 * 512], BF16))
        mm_t = e(nc.sbuf_tensor("q_mm", [128, 2 * 512], BF16))
        tau_t = e(nc.sbuf_tensor("q_tau", [128, 2 * 512], BF16))
        whb = e(nc.sbuf_tensor("q_whb", [128, 16 * 512], BF16))
        ps_T = [e(nc.psum_tensor(f"q_psT{k}", [128, 1024], BF16)) for k in range(3)]
        ps_G = e(nc.psum_tensor("q_psG", [32, 512], F32))
        ps_S = [e(nc.psum_tensor(f"q_psS{k}", [128, 512], F32)) for k in range(2)]
        ps_TH = [e(nc.psum_tensor(f"q_psTH{k}", [128, 512], F32)) for k in range(2)]

        natv = nat[:, :].rearrange("p (r f) -> p r f", r=4)
        natbv = natb[:, :].rearrange("p (sp r f) -> p sp r f", sp=2, r=4)
        aTv_ = aT[:, :].rearrange("p (sp i f) -> p sp i f", sp=2, f=512)
        sigv_ = sig[:, :].rearrange("p (sp i f) -> p sp i f", sp=2, f=512)
        sgv = sg[:, :].rearrange("p (sp f) -> p sp f", sp=2)
        thfv = thf[:, :].rearrange("p (sp f) -> p sp f", sp=2)
        thbv = thb[:, :].rearrange("p (sp f) -> p sp f", sp=2)
        mmv = mm_t[:, :].rearrange("p (k f) -> p k f", k=2)
        tauv = tau_t[:, :].rearrange("p (k f) -> p k f", k=2)

        K_ITER = None

        class SetCtx:
            pass

        def emit_pipeline(sets):
            nsets = len(sets)
            prevS0 = 0
            sig_freeG = [[0] * 16, [0] * 16]
            aT_freeV = [0, 0]
            aT_freeT = [0, 0]
            whb_free = 0
            psS_carry = [0, 0]
            gsum_last = 0
            sg_read = 0
            slot_free = [0, 0]
            nat_dls = {}
            cast_vc = {}
            cast_sc = {}
            prev_emb = False

            def emit_load(kk):
                if kk >= nsets:
                    return
                qn = b.ds(f"qnat{kk % 2}")
                cr = (cast_vc.get(kk - 1, 0), cast_sc.get(kk - 1, 0))

                @blk.sync
                def _(sy, cur=sets[kk], qn=qn, cr=cr):
                    if cr[0] > 0:
                        sy.wait_ge(b.vE.h, cr[0])
                    if cr[1] > 0:
                        sy.wait_ge(b.sE.h, cr[1])
                    sy.dma_start(out=natv[:, :, :],
                                 in_=cur.load_ap).then_inc(qn.h, 16)
                qn.inc(16)
                nat_dls[kk] = (qn.h, qn.n)

            def emit_vcasts(kk):
                if kk >= nsets:
                    return
                sl = kk % 2
                nat_dl = nat_dls[kk]

                @blk.vector
                def _(v, sl=sl, nat_dl=nat_dl, sf=slot_free[sl]):
                    v.wait_ge(nat_dl[0], nat_dl[1])
                    if sf > 0:
                        v.wait_ge(b.tE.h, sf)
                    v.tensor_copy(natbv[:, sl, 0, :], natv[:, 0, :]).then_inc(b.vE.h, 1)
                    v.tensor_copy(natbv[:, sl, 1, :], natv[:, 1, :]).then_inc(b.vE.h, 1)
                b.vE.inc(2)
                cast_vc[kk] = b.vE.n

            emit_load(0)
            emit_vcasts(0)

            @blk.scalar
            def _(s, nat_dl=nat_dls[0]):
                s.wait_ge(nat_dl[0], nat_dl[1])
                s.activation(natbv[:, 0, 2, :], natv[:, 2, :],
                             AF.Copy).then_inc(b.sE.h, 1)
                s.activation(natbv[:, 0, 3, :], natv[:, 3, :],
                             AF.Copy).then_inc(b.sE.h, 1)
            b.sE.inc(2)
            cast_sc[0] = b.sE.n
            for k in range(nsets + 1):
                cur = sets[k] if k < nsets else None
                prv = sets[k - 1] if k >= 1 else None
                T0, S0, V0, G0 = b.tE.n, b.sE.n, b.vE.n, b.gE.n
                sp = k % 2
                pv = (k - 1) % 2
                gsum_prev = gsum_last
                emit_load(k + 1)


                emb_k = (cur is not None) and (k + 1 < nsets)

                def s_abs(S0_, pk, emb):
                    return S0_ + 2 * pk + 1 + (2 if (emb and pk >= 4) else 0)

                def s_sign(S0_, pk, emb):
                    return s_abs(S0_, pk, emb) + 1

                v_base = V0 + (4 if prv is not None else 0)

                def v_m(i):
                    return v_base + 3 * i + 1

                def v_tau(i):
                    return v_base + 3 * i + 2

                def v_wh(i):
                    return v_base + 3 * i + 3

                tr_rec = {}
                bc_rec = {}
                gs_rec = {}
                sl = k % 2

                def emit_tr(pk):
                    @blk.tensor
                    def _(t, pk=pk, sl=sl, vck=cast_vc[k], sck=cast_sc[k],
                          pS0=prevS0, S0=S0, emb=emb_k, pemb=prev_emb):
                        if pk == 0:
                            t.wait_ge(b.vE.h, vck)
                            t.wait_ge(b.sE.h, sck)
                        if pk >= 3:
                            t.wait_ge(b.sE.h, s_sign(S0, pk - 3, emb))
                        elif pS0 > 0:
                            t.wait_ge(b.sE.h,
                                      s_sign(pS0, {0: 6, 1: 7, 2: 5}[pk], pemb))
                        for ii in (2 * pk, 2 * pk + 1):
                            for r in range(4):
                                _li = t.transpose(
                                    ps_T[pk % 3][:, (ii % 2) * 512 + r * 128:
                                                 (ii % 2) * 512 + (r + 1) * 128],
                                    natbv[:, sl, r, ii * 128:(ii + 1) * 128],
                                    C["idn"][:, :])
                        _li.then_inc(b.tE.h, 1)
                    tr_rec[pk] = b.tE.inc()

                def emit_gsum(i):
                    @blk.tensor
                    def _(t, i=i, sp=sp, sgr=sg_read, S0=S0, emb=emb_k,
                          aTv=aTv_[:, sp]):
                        t.wait_ge(b.sE.h, s_sign(S0, 7, emb) if i == 15
                                  else s_abs(S0, i // 2, emb))
                        if i == 0:
                            if sgr > 0:
                                t.wait_ge(b.vE.h, sgr)
                        t.matmul(ps_G[:, :], C["gselv"][:, i, :], aTv[:, i, :],
                                 start=(i == 0), stop=(i == 15)).then_inc(b.tE.h, 1)
                    gs_rec[i] = b.tE.inc()

                def emit_bc(i):
                    pp = i % 2

                    @blk.tensor
                    def _(t, i=i, pp=pp, V0=V0, pv=pv, psc=psS_carry[i % 2],
                          v_base=v_base):
                        t.wait_ge(b.vE.h, V0 + 4)
                        if i >= 2:
                            t.wait_ge(b.vE.h, v_wh(i - 2))
                        elif psc > 0:
                            t.wait_ge(b.vE.h, psc)
                        t.matmul(ps_S[pp][:, :], C["bselv"][:, i, :], sgv[:, pv, :],
                                 start=True, stop=True)
                        t.matmul(ps_TH[pp][:, :], C["bselv"][:, i, :], thbv[:, pv, :],
                                 start=True, stop=True).then_inc(b.tE.h, 1)
                    bc_rec[i] = b.tE.inc()

                if cur is not None and prv is not None:
                    for pk in range(8):
                        emit_tr(pk)
                        for j in (2 * pk - 2, 2 * pk - 1):
                            if 0 <= j < 16:
                                emit_bc(j)
                        for j in (2 * pk - 4, 2 * pk - 3):
                            if 0 <= j < 16:
                                emit_gsum(j)
                    emit_bc(14)
                    emit_bc(15)
                    for i in (12, 13, 14, 15):
                        emit_gsum(i)
                elif cur is not None:
                    for pk in range(8):
                        emit_tr(pk)
                        for j in (2 * pk - 4, 2 * pk - 3):
                            if 0 <= j < 16:
                                emit_gsum(j)
                    for i in (12, 13, 14, 15):
                        emit_gsum(i)
                elif prv is not None:
                    for i in range(16):
                        emit_bc(i)
                if cur is not None:
                    gsum_last = gs_rec[15]
                    slot_free[sl] = tr_rec[7]

                if cur is not None:
                    @blk.scalar
                    def _(s, S0=S0, sp=sp, kk=k, atV=aT_freeV[sp], emb=emb_k,
                          trp=[tr_rec[pk] for pk in range(8)],
                          sgf=list(sig_freeG[sp]), sigv=sigv_[:, sp],
                          aTv=aTv_[:, sp]):
                        for pk in range(8):
                            s.wait_ge(b.tE.h, trp[pk])
                            if pk == 0 and atV > 0:
                                s.wait_ge(b.vE.h, atV)
                            li = s.activation(aTv[:, 2 * pk:2 * pk + 2, :],
                                              ps_T[pk % 3][:, :].rearrange(
                                                  "p (a f) -> p a f", a=2), AF.Abs)
                            li.then_inc(b.sE.h, 1)
                            if sgf[2 * pk + 1] > 0:
                                s.wait_ge(b.vE.h, sgf[2 * pk + 1])
                            s.activation(sigv[:, 2 * pk:2 * pk + 2, :],
                                         ps_T[pk % 3][:, :].rearrange(
                                             "p (a f) -> p a f", a=2),
                                         AF.Sign).then_inc(b.sE.h, 1)
                            if pk == 3 and emb:
                                sl2 = (kk + 1) % 2
                                nd = nat_dls[kk + 1]
                                s.wait_ge(nd[0], nd[1])
                                s.activation(natbv[:, sl2, 2, :], natv[:, 2, :],
                                             AF.Copy).then_inc(b.sE.h, 1)
                                s.activation(natbv[:, sl2, 3, :], natv[:, 3, :],
                                             AF.Copy).then_inc(b.sE.h, 1)
                    b.sE.inc(16 + (2 if emb_k else 0))
                    if emb_k:
                        cast_sc[k + 1] = S0 + 10
                    assert b.sE.n == s_sign(S0, 7, emb_k)

                if prv is not None:
                    @blk.vector
                    def _(v, gsl=gsum_prev, pv=pv):
                        v.wait_ge(b.tE.h, gsl)
                        li = v.tensor_scalar(sgv[:, pv, :], ps_G[:, :], 1.0 / GS, 1e-8,
                                             op0=ALU.mult, op1=ALU.max)
                        b.edge(v, li, "vE")
                        li = v.tensor_scalar(thfv[:, pv, :], sgv[:, pv, :],
                                             0.5 + 2.0 ** -10, None, op0=ALU.mult)
                        b.edge(v, li, "vE")
                        li = v.tensor_single_scalar(thfv[:, pv, :].bitcast(mybir.dt.int32),
                                                    thfv[:, pv, :].bitcast(mybir.dt.int32),
                                                    -65536, op=ALU.bitwise_and)
                        b.edge(v, li, "vE")
                        v.tensor_copy(thbv[:, pv, :], thfv[:, pv, :]).then_inc(b.vE.h, 1)
                    b.vE.inc()
                    sg_read = V0 + 4
                    assert b.vE.n == v_base

                    for i in range(16):
                        pp = i % 2

                        @blk.vector
                        def _(v, i=i, pp=pp, pv=pv, pS0=prevS0, pemb=prev_emb,
                              whf=whb_free, prv=prv, aTv=aTv_[:, pv],
                              sigv=sigv_[:, pv], bcr=bc_rec[i]):
                            v.wait_ge(b.tE.h, bcr)
                            v.tensor_tensor(mmv[:, pp, :], aTv[:, i, :],
                                            ps_TH[pp][:, :],
                                            op=ALU.is_gt).then_inc(b.vE.h, 1)
                            v.wait_ge(b.vE.h, v_m(i))
                            v.wait_ge(b.sE.h, s_sign(pS0, i // 2, pemb))
                            v.tensor_mul(tauv[:, pp, :], mmv[:, pp, :],
                                         sigv[:, i, :]).then_inc(b.vE.h, 1)
                            v.wait_ge(b.vE.h, v_tau(i))
                            if i == 0 and whf > 0:
                                v.wait_ge(b.ds("qwh").h, whf)
                            v.tensor_mul(prv.wh_out(i),
                                         tauv[:, pp, :].rearrange("p (a f) -> p a f", a=prv.wh_out_a),
                                         ps_S[pp][:, :].rearrange("p (a f) -> p a f", a=prv.wh_out_a)
                                         ).then_inc(b.vE.h, 1)
                        b.vE.inc(3)
                        assert b.vE.n == v_wh(i)
                        sig_freeG[pv][i] = v_tau(i)
                    psS_carry = [v_wh(14), v_wh(15)]
                    aT_freeV[pv] = v_m(15)
                    emit_vcasts(k + 1)
                elif cur is not None:
                    emit_vcasts(k + 1)

                if cur is not None:
                    aT_freeT[sp] = gsum_last
                if prv is not None:
                    v_whb = v_wh(15)

                    @blk.sync
                    def _(sy, prv=prv, v_whb=v_whb):
                        sy.wait_ge(b.vE.h, v_whb)
                        prv.cache_write(sy)
                    whb_free = b.ds("qwh").n
                if cur is not None:
                    prevS0 = S0
                    prev_emb = emb_k


        sets = []
        wgu_re = d["wgu"].rearrange("(c r p) dd -> c p r dd", r=4, p=128)
        whb_gu = whb[:, :].rearrange("p (cc i f) -> p cc i f", cc=2, i=16)
        for c in range(32):
            sc = SetCtx()
            sc.load_ap = wgu_re[c]
            sc.wh_out_a = 2
            sc.sp = len(sets) % 2
            sc.wh_out = (lambda i: whb_gu[:, :, i, :])

            def cw_gu(sy, c=c):
                for cc in range(2):
                    sy.dma_start(out=d["cache_gu"][2 * c + cc],
                                 in_=whb_gu[:, cc, :, :]).then_inc(b.ds("qwh").h, 16)
                    b.ds("qwh").inc(16)
            sc.cache_write = cw_gu
            sets.append(sc)

        wp_re = d["wp"].rearrange("(c r p) hh -> c p r hh", r=4, p=128)
        whb_wp = whb[:, :].rearrange("p (ds jj f) -> p ds jj f", ds=4, jj=16)
        cwp = d["cache_wp"].rearrange("ds p (j f) -> ds p j f", f=128)
        for c in range(4):
            for s_ in range(4):
                sc = SetCtx()
                sc.load_ap = wp_re[c][:, :, 2048 * s_: 2048 * (s_ + 1)]
                sc.wh_out_a = 4
                sc.sp = len(sets) % 2
                sc.wh_out = (lambda jj: whb_wp[:, :, jj, :])

                def cw_wp(sy, c=c, s_=s_):
                    for ds in range(4):
                        sy.dma_start(out=cwp[4 * c + ds][:, 16 * s_:16 * s_ + 16, :],
                                     in_=whb_wp[:, ds, :, :]).then_inc(b.ds("qwh").h, 16)
                        b.ds("qwh").inc(16)
                sc.cache_write = cw_wp
                sets.append(sc)

        emit_pipeline(sets)


def emit_ffn(b, d, C, which):
    blk = b.blk
    nc = b.nc
    ntiles = 3 if which == 1 else 2
    src_dram = d["x_feat"] if which == 1 else d["xatt"]
    dst_dram = d["xmid"] if which == 1 else d["out_feat"]
    pref = f"f{which}_"
    with ExitStack() as fs:
        e = fs.enter_context
        xtile = e(nc.sbuf_tensor(pref + "xt", [128, NDC * TT], F32))
        xn = e(nc.sbuf_tensor(pref + "xn", [128, NDC * TT], BF16))
        hbuf = e(nc.sbuf_tensor(pref + "h", [128, 64 * TT], BF16))
        wgub = e(nc.sbuf_tensor(pref + "wgu", [128, 2 * 2 * 16 * 256], BF16))
        wpb = e(nc.sbuf_tensor(pref + "wp", [128, 2 * 64 * 128], BF16))
        obuf = e(nc.sbuf_tensor(pref + "ob", [128, 2 * TT], F32))

        xnv = xn[:, :].rearrange("p (i t) -> p i t", t=TT)
        hv = hbuf[:, :].rearrange("p (j t) -> p j t", t=TT)
        wguv = wgub[:, :].rearrange("p (sl gu i f) -> p sl gu i f", sl=2, gu=2, i=16)
        wpv = wpb[:, :].rearrange("p (sl j f) -> p sl j f", sl=2, j=64)
        obv = obuf[:, :].rearrange("p (k t) -> p k t", k=2)

        wgu_free = [0, 0]
        wp_free = [0, 0]
        gu_free = [0, 0, 0, 0]
        gb_free = [0, 0]
        y_free = [0, 0, 0, 0]
        ob_free = [0, 0]
        h2_cons = [0, 0]
        xres_free = [0, 0, 0]
        phase_bar = b.bar()
        ln_carry = {}

        def xsrc(i):
            return xtile[:, :].rearrange("p (i t) -> p i t", t=TT)[:, i, :]

        def setup_ln_bufs(stk, tag, h2buf=None):
            el = stk.enter_context
            C["ln_xbf"] = wgub[:, 0:2 * NDC * TT]
            if h2buf is None:
                C["ln_ubuf"] = el(nc.sbuf_tensor(pref + f"lu{tag}", [128, TT], F32))
                sm = el(nc.sbuf_tensor(pref + f"lm{tag}", [1, 4 * TT], F32))
                C["ln_msb"] = sm[0:1, 0 * TT:1 * TT]
                C["ln_rssb"] = sm[0:1, 1 * TT:2 * TT]
                C["ln_m2"] = sm[0:1, 2 * TT:3 * TT]
                C["ln_e2"] = sm[0:1, 3 * TT:4 * TT]
            else:
                h2b, mssb = h2buf
                C["ln_ubuf"] = h2b[:, 0:2 * TT].bitcast(F32)
                C["ln_msb"] = h2b[0:1, 2 * TT:4 * TT].bitcast(F32)
                C["ln_m2"] = h2b[0:1, 4 * TT:6 * TT].bitcast(F32)
                C["ln_e2"] = h2b[0:1, 6 * TT:8 * TT].bitcast(F32)
                C["ln_rssb"] = mssb[0:1, 0:TT]
            C["ln_ps_sx"] = el(nc.psum_tensor(pref + f"lpx{tag}", [128, TT], F32))
            C["ln_ps_sxx"] = el(nc.psum_tensor(pref + f"lpy{tag}", [128, TT], F32))
            C["ln_ps_M"] = el(nc.psum_tensor(pref + f"lpm{tag}", [128, TT], F32))
            C["ln_ps_RS"] = el(nc.psum_tensor(pref + f"lpr{tag}", [128, TT], F32))

        for ti in range(ntiles):
            t0 = ti * TT

            if ti == 0:
                fxt = b.ds(pref + "xt")

                @blk.sync
                def _(sy, t0=t0, fxt=fxt):
                    b.wait_bar(sy, phase_bar)
                    sy.dma_start(out=xtile[:, :].rearrange("p (i t) -> p i t", t=TT),
                                 in_=src_dram.rearrange("(i p) t -> p i t", p=128)[:, :, t0:t0 + TT]
                                 ).then_inc(fxt.h, 16)
                fxt.inc(16)

                with ExitStack() as lns:
                    setup_ln_bufs(lns, f"{ti}")
                    stages, lnout = emit_ln_stages(b, C, xsrc,
                                                   lambda i: xnv[:, i, :], TT, b.bar())
                    for st in stages:
                        st()
                v_ln = lnout["v_ln"]
                t_stats = lnout["t_stats"]
            else:
                v_ln = ln_carry[ti]["v_ln"]
                t_stats = ln_carry[ti]["t_stats"]

            with ExitStack() as gs_:
                eg = gs_.enter_context
                ps_g = [eg(nc.psum_tensor(pref + f"pg{ti}_{k}", [128, TT], F32)) for k in range(4)]
                ps_u = [eg(nc.psum_tensor(pref + f"pu{ti}_{k}", [128, TT], F32)) for k in range(4)]
                gbuf = eg(nc.sbuf_tensor(pref + f"g{ti}", [128, 4 * TT], BF16))
                ubuf2 = eg(nc.sbuf_tensor(pref + f"u2{ti}", [128, 2 * TT], BF16))
                gbv = gbuf[:, :].rearrange("p (k t) -> p k t", k=4)
                ubv = ubuf2[:, :].rearrange("p (k t) -> p k t", k=2)

                tA = b.tE.n
                vA = b.vE.n
                sA = b.sE.n

                def t_gu_f(hc):
                    return tA + hc + 1

                def v_h_f(j):
                    return vA + 2 * (j + 1)

                def s_g_f(j):
                    return sA + j + 1

                @blk.scalar
                def _(s, tA=tA, vA=vA, gbA=list(gb_free)):
                    for hc in range(32):
                        for hs in range(2):
                            j = 2 * hc + hs
                            s.wait_ge(b.tE.h, t_gu_f(hc))
                            if j >= 4:
                                s.wait_ge(b.vE.h, v_h_f(j - 4))
                            elif gbA[j % 2] > 0:
                                s.wait_ge(b.vE.h, gbA[j % 2])
                            s.activation(gbv[:, j % 4, :], ps_g[2 * (hc % 2) + hs][:, :],
                                         AF.Silu).then_inc(b.sE.h, 1)
                b.sE.inc(64)
                assert b.sE.n == s_g_f(63)

                for hc in range(32):
                    par = hc % 2
                    sl = par

                    fwg = b.ds(pref + f"wgu{sl}")

                    @blk.sync
                    def _(sy, hc=hc, sl=sl, ti=ti, fwg=fwg,
                          wf=max(wgu_free[hc % 2], t_stats if hc < 2 else 0)):
                        if ti == 0 and hc == 0:
                            b.wait_bar(sy, phase_bar)
                        sy.wait_ge(b.tE.h, wf)
                        sy.dma_start(out=wguv[:, sl, 0, :, :],
                                     in_=d["cache_gu"][hc].rearrange("p (i f) -> p i f", f=256)
                                     ).then_inc(fwg.h, 16)
                        fwg.inc(16)
                        sy.dma_start(out=wguv[:, sl, 1, :, :],
                                     in_=d["cache_gu"][32 + hc].rearrange("p (i f) -> p i f", f=256)
                                     ).then_inc(fwg.h, 16)
                        fwg.inc(16)
                    w_dl = (fwg.h, fwg.n)

                    @blk.tensor
                    def _(t, sl=sl, par=par, w_dl=w_dl, v_ln=v_ln, hc=hc):
                        t.wait_ge(w_dl[0], w_dl[1])
                        vw = max(v_ln, gu_free[2 * par], gu_free[2 * par + 1])
                        t.wait_ge(b.vE.h, vw)
                        for gu in range(2):
                            for hs in range(2):
                                ps = (ps_g if gu == 0 else ps_u)[2 * par + hs]
                                for i in range(NDC):
                                    _li = t.matmul(ps[:, :],
                                             wguv[:, sl, gu, i, hs * 128:(hs + 1) * 128],
                                             xnv[:, i, :],
                                             start=(i == 0), stop=(i == NDC - 1))
                        _li.then_inc(b.tE.h, 1)
                    wgu_free[sl] = b.tE.inc()
                    t_gu = wgu_free[sl]
                    assert t_gu == t_gu_f(hc)

                    for hs in range(2):
                        j = 2 * hc + hs

                        @blk.vector
                        def _(v, par=par, hs=hs, j=j, t_gu=t_gu):
                            v.wait_ge(b.tE.h, t_gu)
                            li = v.tensor_copy(ubv[:, hs, :], ps_u[2 * par + hs][:, :])
                            b.edge(v, li, "vE")
                            v.wait_ge(b.sE.h, s_g_f(j))
                            v.tensor_mul(hv[:, j, :], gbv[:, j % 4, :],
                                         ubv[:, hs, :]).then_inc(b.vE.h, 1)
                        gu_free[2 * par + hs] = b.vE.inc()
                        assert gu_free[2 * par + hs] == v_h_f(j)
                gb_free[0] = b.vE.n
                gb_free[1] = b.vE.n

            with ExitStack() as ps_:
                ep = ps_.enter_context
                ps_ss = ep(nc.psum_tensor(pref + f"pss{ti}", [128, TT], F32))
                ps_y = [ep(nc.psum_tensor(pref + f"py{ti}_{k}", [128, TT], F32)) for k in range(3)]
                h2buf = ep(nc.sbuf_tensor(pref + f"h2{ti}", [128, 8 * TT], BF16))
                mss = ep(nc.sbuf_tensor(pref + f"mss{ti}", [1, TT], F32))
                rssb_b = ep(nc.sbuf_tensor(pref + f"rsb{ti}", [128, TT], F32))
                xres = ep(nc.sbuf_tensor(pref + f"xr{ti}", [128, 3 * TT], F32))
                h2v = h2buf[:, :].rearrange("p (k t) -> p k t", k=8)
                xresv = xres[:, :].rearrange("p (k t) -> p k t", k=3)

                v_h_all = b.vE.n
                v_h2s = {}
                xres_dls = {}

                def emit_h2(qq):
                    if qq >= 16:
                        return

                    @blk.vector
                    def _(v, qq=qq, v_h_all=v_h_all, hc=h2_cons[qq % 2]):
                        v.wait_ge(b.vE.h, v_h_all)
                        if hc > 0:
                            v.wait_ge(b.tE.h, hc)
                        for jj in range(4):
                            li = v.tensor_mul(h2v[:, (qq % 2) * 4 + jj, :],
                                              hv[:, 4 * qq + jj, :],
                                              hv[:, 4 * qq + jj, :])
                        li.then_inc(b.vE.h, 1)
                    v_h2s[qq] = b.vE.inc()

                def emit_xres(ds):
                    if ds >= NDC:
                        return
                    sl4 = ds % 3
                    fxr = b.ds(pref + f"xr{sl4}")

                    @blk.sync
                    def _(sy, ds=ds, sl4=sl4, fxr=fxr, wf=xres_free[sl4], t0=t0):
                        if wf > 0:
                            sy.wait_ge(b.vE.h, wf)
                        sy.dma_start(out=xresv[:, sl4, :],
                                     in_=src_dram.rearrange("(i p) t -> p i t", p=128)[:, ds, t0:t0 + TT]
                                     ).then_inc(fxr.h, 16)
                    fxr.inc(16)
                    xres_dls[ds] = (fxr.h, fxr.n)

                wp_dls = {}

                def emit_wp_load(ds):
                    if ds >= NDC:
                        return
                    sl = ds % 2
                    fwp = b.ds(pref + f"wp{sl}")
                    fwp2 = b.ds(pref + f"wp{sl}b")
                    wf = wp_free[sl]
                    src = d["cache_wp"][ds].rearrange("p (j f) -> p j f", f=128)

                    @blk.sync
                    def _(sy, sl=sl, wf=wf, fwp=fwp, src=src):
                        sy.wait_ge(b.tE.h, wf)
                        sy.dma_start(out=wpv[:, sl, 0:32, :],
                                     in_=src[:, 0:32, :]).then_inc(fwp.h, 16)
                    fwp.inc(16)
                    if ds < 2:
                        @blk.sync
                        def _(sy, sl=sl, wf=wf, fwp2=fwp2, src=src):
                            sy.dma_start(out=wpv[:, sl, 32:64, :],
                                         in_=src[:, 32:64, :]).then_inc(fwp2.h, 16)
                        fwp2.inc(16)
                    else:
                        @blk.scalar
                        def _(s, sl=sl, wf=wf, fwp2=fwp2, src=src):
                            s.wait_ge(b.tE.h, wf)
                            s.dma_start(out=wpv[:, sl, 32:64, :],
                                        in_=src[:, 32:64, :]).then_inc(fwp2.h, 16)
                        fwp2.inc(16)
                    wp_dls[ds] = ((fwp.h, fwp.n), (fwp2.h, fwp2.n))

                emit_wp_load(0)
                emit_wp_load(1)
                emit_xres(0)
                emit_xres(1)
                emit_xres(2)
                emit_h2(0)
                emit_h2(1)

                def emit_ss(qq):
                    @blk.tensor
                    def _(t, qq=qq, v_h2=v_h2s[qq]):
                        t.wait_ge(b.vE.h, v_h2)
                        for jj in range(4):
                            _li = t.matmul(ps_ss[0:1, 0:TT], C["onesb"][:, :],
                                           h2v[:, (qq % 2) * 4 + jj, :],
                                           start=(qq == 0 and jj == 0),
                                           stop=(qq == 15 and jj == 3))
                        _li.then_inc(b.tE.h, 1)
                    h2_cons[qq % 2] = b.tE.inc()

                def emit_rs():
                    @blk.vector
                    def _(v, t_ss=h2_cons[15 % 2]):
                        v.wait_ge(b.tE.h, t_ss)
                        v.tensor_scalar(mss[0:1, 0:TT], ps_ss[0:1, 0:TT], 1.0 / HFF, EPS_RMS,
                                        op0=ALU.mult, op1=ALU.add).then_inc(b.vE.h, 1)
                    v_ms = b.vE.inc()

                    @blk.scalar
                    def _(s, v_ms=v_ms):
                        s.wait_ge(b.vE.h, v_ms)
                        li = s.activation(mss[0:1, 0:TT], mss[0:1, 0:TT], AF.Ln)
                        b.edge(s, li, "sE")
                        s.activation(mss[0:1, 0:TT], mss[0:1, 0:TT], AF.Exp,
                                     scale=-0.5).then_inc(b.sE.h, 1)
                    s_rs = b.sE.inc()

                    @blk.tensor
                    def _(t, s_rs=s_rs):
                        t.wait_ge(b.sE.h, s_rs)
                        _li = t.matmul(ps_ss[:, 0:TT], C["onesf"][0:1, :], mss[0:1, 0:TT],
                                 start=True, stop=True).then_inc(b.tE.h, 1)
                    t_rb = b.tE.inc()

                    @blk.vector
                    def _(v, t_rb=t_rb):
                        v.wait_ge(b.tE.h, t_rb)
                        v.tensor_copy(rssb_b[:, 0:TT], ps_ss[:, 0:TT]).then_inc(b.vE.h, 1)
                    return b.vE.inc()

                t_ys = {}
                v_rsb = None

                def emit_epi(ds, v_rsb_):
                    par = ds % 3
                    obp = ds % 2

                    @blk.vector
                    def _(v, ds=ds, par=par, obp=obp, t_y=t_ys[ds], v_rsb_=v_rsb_,
                          xr_dl=xres_dls[ds], obf=ob_free[obp]):
                        v.wait_ge(b.tE.h, t_y)
                        if v_rsb_ is not None:
                            v.wait_ge(b.vE.h, v_rsb_)
                        v.wait_ge(xr_dl[0], xr_dl[1])
                        if obf > 0:
                            v.wait_ge(b.ds(pref + f"ob{obp}").h, obf)
                        li = v.scalar_tensor_tensor(obv[:, obp, :], ps_y[par][:, 0:TT], 0.5,
                                                    rssb_b[:, 0:TT], op0=ALU.mult, op1=ALU.mult)
                        b.edge(v, li, "vE")
                        v.tensor_add(obv[:, obp, :], obv[:, obp, :],
                                     xresv[:, ds % 3, :]).then_inc(b.vE.h, 1)
                    y_free[par] = b.vE.inc()
                    xres_free[ds % 3] = y_free[par]
                    emit_xres(ds + 3)
                    v_o = y_free[par]

                    fob = b.ds(pref + f"ob{obp}")

                    @blk.sync
                    def _(sy, ds=ds, obp=obp, v_o=v_o, t0=t0, fob=fob):
                        sy.wait_ge(b.vE.h, v_o)
                        dv = dst_dram.rearrange("(i p) t -> p i t", p=128)
                        sy.dma_start(out=dv[:, ds, t0:t0 + TT],
                                     in_=obv[:, obp, :]).then_inc(fob.h, 16)
                    fob.inc(16)
                    ob_free[obp] = fob.n

                ln_next = None
                for ds in range(NDC):
                    par = ds % 3
                    sl = ds % 2
                    wp_dl = wp_dls[ds]

                    @blk.tensor
                    def _(t, ds=ds, sl=sl, par=par, wp_dl=wp_dl,
                          yf=max(y_free[par], v_h_all)):
                        t.wait_ge(wp_dl[0][0], wp_dl[0][1])
                        t.wait_ge(wp_dl[1][0], wp_dl[1][1])
                        t.wait_ge(b.vE.h, yf)
                        for j in range(64):
                            _li = t.matmul(ps_y[par][:, 0:TT], wpv[:, sl, j, :], hv[:, j, :],
                                     start=(j == 0), stop=(j == 63))
                        _li.then_inc(b.tE.h, 1)
                    wp_free[sl] = b.tE.inc()
                    t_ys[ds] = wp_free[sl]
                    emit_wp_load(ds + 2)

                    if ds == 0:
                        for k4 in range(4):
                            emit_ss(k4)
                            emit_h2(k4 + 2)
                    elif ds == 1:
                        for k4 in range(4, 9):
                            emit_ss(k4)
                            emit_h2(k4 + 2)
                    elif ds == 2:
                        for k4 in range(9, 16):
                            emit_ss(k4)
                            emit_h2(k4 + 2)
                        v_rsb = emit_rs()
                        for dd in range(3):
                            emit_epi(dd, v_rsb)
                    else:
                        emit_epi(ds, v_rsb)

                    if ti + 1 < ntiles:
                        if ds == 5:
                            fxt = b.ds(pref + "xt")

                            @blk.sync
                            def _(sy, t0n=t0 + TT, fxt=fxt, v_ln=v_ln):
                                sy.wait_ge(b.vE.h, v_ln)
                                sy.dma_start(
                                    out=xtile[:, :].rearrange("p (i t) -> p i t", t=TT),
                                    in_=src_dram.rearrange("(i p) t -> p i t", p=128)[:, :, t0n:t0n + TT]
                                    ).then_inc(fxt.h, 16)
                            fxt.inc(16)
                        elif ds == 6:
                            setup_ln_bufs(ps_, f"{ti}n", h2buf=(h2buf, mss))
                            ln_stages, ln_out = emit_ln_stages(
                                b, C, xsrc, lambda i: xnv[:, i, :], TT, b.bar())
                            ln_stages[0]()
                        elif ds == 8:
                            ln_stages[1]()
                        elif ds == 9:
                            ln_stages[2]()
                        elif ds == 10:
                            ln_stages[3]()
                        elif ds == 11:
                            ln_stages[4]()
                            ln_carry[ti + 1] = dict(v_ln=ln_out["v_ln"],
                                                    t_stats=ln_out["t_stats"])



def emit_attn(b, d, C):
    blk = b.blk
    nc = b.nc
    with ExitStack() as as_:
        e = as_.enter_context
        kF = e(nc.sbuf_tensor("a_kF", [128, NKV * TLOC], BF16))
        vsb = e(nc.sbuf_tensor("a_v", [128, 12 * 512], BF16))
        oF = e(nc.sbuf_tensor("a_oF", [128, NH * TOWN], BF16))
        sb_rope = e(nc.sbuf_tensor("a_rope", [128, 2 * 12 * 64], F32))
        C["cosv"] = sb_rope[:, 0:768].rearrange("p (ts i) -> p ts i", i=64)
        C["sinv"] = sb_rope[:, 768:1536].rearrange("p (ts i) -> p ts i", i=64)
        xn2own = e(nc.sbuf_tensor("a_xn2", [128, NDC * TOWN], BF16))

        kFv = kF[:, :].rearrange("p (h j) -> p h j", j=TLOC)
        vv = vsb[:, :].rearrange("p (ts e) -> p ts e", e=512)
        oFv = oF[:, :].rearrange("p (h a) -> p h a", a=TOWN)
        xn2v = xn2own[:, :].rearrange("p (i t) -> p i t", t=TOWN)

        rope_bar = b.bar()

        rps = b.ds("rope")

        @blk.sync
        def _(sy):
            sy.dma_start(out=C["cosv"],
                         in_=d["rope_cos"].rearrange("(ts p) i -> p ts i", p=128)).then_inc(rps.h, 16)
            rps.inc(16)
            sy.dma_start(out=C["sinv"],
                         in_=d["rope_sin"].rearrange("(ts p) i -> p ts i", p=128)).then_inc(rps.h, 16)
            rps.inc(16)
        rope_dl = (rps.h, rps.n)

        with ExitStack() as s1:
            e1 = s1.enter_context
            wkT = e1(nc.sbuf_tensor("a_wkT", [128, 16 * 512], BF16))
            wvT = e1(nc.sbuf_tensor("a_wvT", [128, 16 * 512], BF16))
            wkTv = wkT[:, :].rearrange("p (i f) -> p i f", f=512)
            wvTv = wvT[:, :].rearrange("p (i f) -> p i f", f=512)

            with ExitStack() as pw:
                ew = pw.enter_context
                wnat = ew(nc.sbuf_tensor("a_wnat", [128, 2 * 2048], F32))
                ps_W = [ew(nc.psum_tensor(f"a_ps_W{k}", [128, 2048], F32))
                        for k in range(2)]
                wnatv = wnat[:, :].rearrange("p (r f) -> p r f", r=2)
                psW_free = [0, 0]
                wnat_free = [0, 0]

                units = [(nm, dstv, r) for nm, dstv in (("wk", wkTv), ("wv", wvTv))
                         for r in range(4)]
                n_dls = {}

                def emit_wl(u):
                    if u >= len(units):
                        return
                    nm, dstv, r = units[u]
                    sl = u % 2
                    wns = b.ds(f"wnat{sl}")

                    @blk.sync
                    def _(sy, nm=nm, r=r, sl=sl, u=u, wns=wns, wf=wnat_free[sl]):
                        if wf > 0:
                            sy.wait_ge(b.tE.h, wf)
                        if u == 0:
                            b.wait_bar(sy, b.bar())
                        sy.dma_start(out=wnatv[:, sl, :],
                                     in_=d[nm].rearrange("(r p) dd -> r p dd",
                                                         p=128)[r]
                                     ).then_inc(wns.h, 16)
                    wns.inc(16)
                    n_dls[u] = (wns.h, wns.n)

                emit_wl(0)
                emit_wl(1)
                for u in range(len(units)):
                    nm, dstv, r = units[u]
                    sl = u % 2

                    @blk.tensor
                    def _(t, sl=sl, n_dl=n_dls[u], pf=psW_free[sl]):
                        t.wait_ge(n_dl[0], n_dl[1])
                        if pf > 0:
                            t.wait_ge(b.vE.h, pf)
                        for i in range(16):
                            _li = t.transpose(ps_W[sl][:, i * 128:(i + 1) * 128],
                                        wnatv[:, sl, i * 128:(i + 1) * 128],
                                        C["idnf"][:, :])
                        _li.then_inc(b.tE.h, 1)
                    t_tr = b.tE.inc()
                    wnat_free[sl] = t_tr
                    emit_wl(u + 2)

                    @blk.vector
                    def _(v, r=r, sl=sl, t_tr=t_tr, dstv=dstv):
                        v.wait_ge(b.tE.h, t_tr)
                        v.tensor_copy(dstv[:, :, r * 128:(r + 1) * 128],
                                      ps_W[sl][:, :].rearrange("p (i f) -> p i f", f=128)
                                      ).then_inc(b.vE.h, 1)
                    psW_free[sl] = b.vE.inc()

            with ExitStack() as s1b:
                eb = s1b.enter_context
                xmt = eb(nc.sbuf_tensor("a_xmt", [128, NDC * TT], F32))
                xmtv = xmt[:, :].rearrange("p (i t) -> p i t", t=TT)
                xnst = eb(nc.sbuf_tensor("a_xnst", [128, NDC * TT], BF16))
                xnstv = xnst[:, :].rearrange("p (i t) -> p i t", t=TT)
                xn_st_free = [0]

                for ti in range(3):
                    t0 = ti * TT
                    bar = b.bar()
                    xms = b.ds("xmt")

                    @blk.sync
                    def _(sy, t0=t0, bar=bar, ti=ti, xms=xms):
                        sy.wait_ge(b.vE.h, bar["v"])
                        if ti == 0:
                            b.wait_bar(sy, bar)
                        sy.dma_start(out=xmtv[:, :, :],
                                     in_=d["xmid"].rearrange("(i p) t -> p i t", p=128)[:, :, t0:t0 + TT]
                                     ).then_inc(xms.h, 16)
                    xms.inc(16)
                    x_dl = (xms.h, xms.n)

                    @blk.vector
                    def _(v, x_dl=x_dl, ti=ti, xf=xn_st_free[0]):
                        v.wait_ge(x_dl[0], x_dl[1])
                        if ti == 0:
                            v.wait_ge(rope_dl[0], rope_dl[1])
                        if xf > 0:
                            v.wait_ge(b.ds("xn2").h, xf)
                        v.engine_nop().then_inc(b.vE.h, 1)
                    b.vE.inc()

                    def xsrc(i):
                        return xmtv[:, i, :]

                    def xn_dst(i):
                        return xnstv[:, i, :]

                    TH = TT // 2
                    with ExitStack() as lns:
                        el = lns.enter_context
                        sts = []
                        outs = []
                        for hf in range(2):
                            C["ln_xbf"] = oF[:, hf * 2 * NDC * TH:(hf + 1) * 2 * NDC * TH]
                            C["ln_ubuf"] = el(nc.sbuf_tensor(f"a_lu{ti}{hf}", [128, TH], F32))
                            C["ln_msb"] = el(nc.sbuf_tensor(f"a_lm{ti}{hf}", [1, TH], F32))
                            C["ln_m2"] = el(nc.sbuf_tensor(f"a_l2{ti}{hf}", [1, TH], F32))
                            C["ln_e2"] = el(nc.sbuf_tensor(f"a_le{ti}{hf}", [1, TH], F32))
                            C["ln_rssb"] = el(nc.sbuf_tensor(f"a_lr{ti}{hf}", [1, TH], F32))
                            C["ln_ps_sx"] = el(nc.psum_tensor(f"a_lpx{ti}{hf}", [128, TT], F32))[:, 0:TH]
                            C["ln_ps_sxx"] = el(nc.psum_tensor(f"a_lpy{ti}{hf}", [128, TT], F32))[:, 0:TH]
                            C["ln_ps_M"] = el(nc.psum_tensor(f"a_lpm{ti}{hf}", [128, TT], F32))[:, 0:TH]
                            C["ln_ps_RS"] = el(nc.psum_tensor(f"a_lpr{ti}{hf}", [128, TT], F32))[:, 0:TH]
                            st, out = emit_ln_stages(
                                b, C,
                                (lambda i, hf=hf: xsrc(i)[:, hf * TH:(hf + 1) * TH]),
                                (lambda i, hf=hf: xn_dst(i)[:, hf * TH:(hf + 1) * TH]),
                                TH, b.bar())
                            sts.append(st)
                            outs.append(out)
                        for stg in range(5):
                            sts[0][stg]()
                            sts[1][stg]()
                        v_ln = outs[1]["v_ln"]

                    xns = b.ds("xn2")

                    @blk.sync
                    def _(sy, t0=t0, v_ln=v_ln, xns=xns):
                        sy.wait_ge(b.vE.h, v_ln)
                        sy.dma_start(out=d["xn2"].rearrange("(i p) t -> p i t", p=128)[:, :, t0:t0 + TT],
                                     in_=xnstv[:, :, :]).then_inc(xns.h, 16)
                    xns.inc(16)
                    xn_st_free[0] = xns.n
            xn2_done = b.ds("xn2").n

            with ExitStack() as s1c:
                ec_ = s1c.enter_context
                xns_b = ec_(nc.sbuf_tensor("a_xnsb", [128, 2 * 16 * 128], BF16))
                xnsv = xns_b[:, :].rearrange("p (k i f) -> p k i f", k=2, i=16)
                ktm = ec_(nc.sbuf_tensor("a_ktm", [128, 12 * 512], BF16))
                ktmv = ktm[:, :].rearrange("p (ts f) -> p ts f", f=512)
                kro = ec_(nc.sbuf_tensor("a_kro", [128, 2 * 512], BF16))
                krov = kro[:, :].rearrange("p (k f) -> p k f", k=2)
                rsc = ec_(nc.sbuf_tensor("a_rsc", [128, 512], F32))
                ps_kv = [ec_(nc.psum_tensor(f"a_pkv{k}", [128, 512], F32)) for k in range(4)]
                ps_T2 = [ec_(nc.psum_tensor(f"a_psT2{k}", [128, 512], BF16)) for k in range(2)]

                kv_free = [0, 0, 0, 0]
                xn_cons = [0, 0]

                for ts in range(12):
                    sl = ts % 2
                    xns2 = b.ds(f"xns{sl}")

                    @blk.sync
                    def _(sy, ts=ts, sl=sl, xns2=xns2, xc=xn_cons[sl], xd=xn2_done):
                        sy.wait_ge(b.ds("xn2").h, xd)
                        if xc > 0:
                            sy.wait_ge(b.tE.h, xc)
                        sy.dma_start(out=xnsv[:, sl, :, :],
                                     in_=d["xn2"].rearrange("(i p) t -> p i t", p=128)[:, :, ts * 128:(ts + 1) * 128]
                                     ).then_inc(xns2.h, 16)
                    xns2.inc(16)
                    xn_dl = (xns2.h, xns2.n)

                    for kv in range(2):
                        wTv_ = wkTv if kv == 0 else wvTv
                        bank = 2 * kv + sl

                        @blk.tensor
                        def _(t, ts=ts, kv=kv, sl=sl, bank=bank, wTv_=wTv_, xn_dl=xn_dl):
                            t.wait_ge(xn_dl[0], xn_dl[1])
                            t.wait_ge(b.vE.h, kv_free[bank])
                            for i in range(NDC):
                                _li = t.matmul(ps_kv[bank][:, :],
                                         xnsv[:, sl, i, :],
                                         wTv_[:, i, :], start=(i == 0), stop=(i == NDC - 1))
                            _li.then_inc(b.tE.h, 1)
                        t_kv = b.tE.inc()
                        if kv == 1:
                            xn_cons[sl] = t_kv

                        @blk.vector
                        def _(v, ts=ts, kv=kv, bank=bank, t_kv=t_kv):
                            v.wait_ge(b.tE.h, t_kv)
                            if kv == 0:
                                v.tensor_copy(ktmv[:, ts, :], ps_kv[bank][:, :]).then_inc(b.vE.h, 1)
                            else:
                                v.tensor_copy(vv[:, ts, :], ps_kv[bank][:, :]).then_inc(b.vE.h, 1)
                        kv_free[bank] = b.vE.inc()

                kro_cons = [0, 0]
                psT2_free = [0, 0]
                for ts in range(12):
                    pp = ts % 2

                    @blk.vector
                    def _(v, ts=ts, pp=pp, kc=kro_cons[pp]):
                        v.wait_ge(b.tE.h, kc)
                        kt = ktmv[:, ts, :].rearrange("p (h i two) -> p h i two", h=4, two=2)
                        kr = krov[:, pp, :].rearrange("p (h i two) -> p h i two", h=4, two=2)
                        cc = C["cosv"][:, ts, :].unsqueeze(1).unsqueeze(3).broadcast_to((128, 4, 64, 1))
                        ss = C["sinv"][:, ts, :].unsqueeze(1).unsqueeze(3).broadcast_to((128, 4, 64, 1))
                        r0 = rsc[:, 0:256].rearrange("p (h i) -> p h i", h=4).unsqueeze(3)
                        r1 = rsc[:, 256:512].rearrange("p (h i) -> p h i", h=4).unsqueeze(3)
                        v.tensor_tensor(r0, kt[:, :, :, 0:1], cc, op=ALU.mult)
                        li = v.tensor_tensor(r1, kt[:, :, :, 1:2], ss, op=ALU.mult)
                        b.edge(v, li, "vE")
                        li = v.tensor_tensor(kr[:, :, :, 0:1], r0, r1, op=ALU.subtract)
                        b.edge(v, li, "vE")
                        v.tensor_tensor(r0, kt[:, :, :, 0:1], ss, op=ALU.mult)
                        li = v.tensor_tensor(r1, kt[:, :, :, 1:2], cc, op=ALU.mult)
                        b.edge(v, li, "vE")
                        v.tensor_tensor(kr[:, :, :, 1:2], r0, r1, op=ALU.add).then_inc(b.vE.h, 1)
                    v_kro = b.vE.inc(1)

                    @blk.tensor
                    def _(t, ts=ts, pp=pp, v_kro=v_kro, ptf=psT2_free[pp]):
                        t.wait_ge(b.vE.h, max(v_kro, ptf))
                        for ecc in range(4):
                            _li = t.transpose(ps_T2[pp][:, ecc * 128:(ecc + 1) * 128],
                                        krov[:, pp, ecc * 128:(ecc + 1) * 128],
                                        C["idn"][:, :])
                        _li.then_inc(b.tE.h, 1)
                    t_ktr = b.tE.inc()
                    kro_cons[pp] = t_ktr

                    @blk.vector
                    def _(v, ts=ts, pp=pp, t_ktr=t_ktr):
                        v.wait_ge(b.tE.h, t_ktr)
                        v.tensor_copy(kFv[:, :, ts * 128:(ts + 1) * 128],
                                      ps_T2[pp][:, :].rearrange("p (h f) -> p h f", f=128)
                                      ).then_inc(b.vE.h, 1)
                    psT2_free[pp] = b.vE.inc()

        with ExitStack() as s2:
            e2_ = s2.enter_context
            qF = e2_(nc.sbuf_tensor("a_qF", [128, NH * TOWN], BF16))
            qFv = qF[:, :].rearrange("p (h a) -> p h a", a=TOWN)
            s2i = e2_(ExitStack())
            e2b = s2i.enter_context
            qtm = e2b(nc.sbuf_tensor("a_qtm", [128, 8 * 2048], BF16))
            qtmv = qtm[:, :].rearrange("p (ts e) -> p ts e", e=2048)
            wqT = e2b(nc.sbuf_tensor("a_wqT", [128, 16 * 512], BF16))
            wqTv = wqT[:, :].rearrange("p (i f) -> p i f", f=512)
            wnat2 = e2b(nc.sbuf_tensor("a_wnat2", [128, 2 * 2048], F32))
            wnat2v = wnat2[:, :].rearrange("p (r f) -> p r f", r=2)
            xns_b2 = e2b(nc.sbuf_tensor("a_xnsb2", [128, 2 * 16 * 128], BF16))
            xnsv2 = xns_b2[:, :].rearrange("p (k i f) -> p k i f", k=2, i=16)
            qro = oF[:, 0:2048]
            rsc2 = oF[:, 2048:4096]
            ps_q = [e2b(nc.psum_tensor(f"a_pq{k}", [128, 512], F32)) for k in range(2)]
            ps_W2 = [e2b(nc.psum_tensor(f"a_psW2{k}", [128, 1024], F32))
                     for k in range(2)]
            ps_T3 = e2b(nc.psum_tensor("a_psT3", [128, 2048], BF16))

            psW2_free = [0, 0]
            wnat2_free = [0, 0]
            q_free = [0, 0]
            qro_cons = [0]
            psT3_free = [0]
            xn_cons2 = [0, 0]
            a2_bar = b.bar()
            wq_dls = {}

            def emit_wql(g):
                if g >= 16:
                    return
                sl = g % 2
                wns2 = b.ds(f"wnat2{sl}")

                @blk.sync
                def _(sy, g=g, sl=sl, wns2=wns2, wf=wnat2_free[sl]):
                    if g == 0:
                        b.wait_bar(sy, a2_bar)
                    if wf > 0:
                        sy.wait_ge(b.tE.h, wf)
                    sy.dma_start(out=wnat2v[:, sl, :],
                                 in_=d["wq"].rearrange("(g p) dd -> g p dd",
                                                       p=128)[g]
                                 ).then_inc(wns2.h, 16)
                wns2.inc(16)
                wq_dls[g] = (wns2.h, wns2.n)

            emit_wql(0)
            emit_wql(1)
            for ec in range(4):
                for r in range(4):
                    g = 4 * ec + r
                    sl = g % 2
                    for hh in range(2):
                        q2 = (2 * g + hh) % 2

                        @blk.tensor
                        def _(t, sl=sl, hh=hh, q2=q2, n_dl=wq_dls[g],
                              pf=psW2_free[q2]):
                            t.wait_ge(n_dl[0], n_dl[1])
                            if pf > 0:
                                t.wait_ge(b.vE.h, pf)
                            for i in range(8):
                                _li = t.transpose(ps_W2[q2][:, i * 128:(i + 1) * 128],
                                            wnat2v[:, sl, hh * 1024 + i * 128:hh * 1024 + (i + 1) * 128],
                                            C["idnf"][:, :])
                            _li.then_inc(b.tE.h, 1)
                        t_tr = b.tE.inc()
                        if hh == 1:
                            wnat2_free[sl] = t_tr
                            emit_wql(g + 2)

                        @blk.vector
                        def _(v, r=r, hh=hh, q2=q2, t_tr=t_tr):
                            v.wait_ge(b.tE.h, t_tr)
                            v.tensor_copy(wqTv[:, 8 * hh:8 * (hh + 1), r * 128:(r + 1) * 128],
                                          ps_W2[q2][:, :].rearrange("p (i f) -> p i f", f=128)
                                          ).then_inc(b.vE.h, 1)
                        psW2_free[q2] = b.vE.inc()
                v_wq = b.vE.n

                for ts in range(8):
                    pp = ts % 2
                    if ec == 0:
                        xnq = b.ds(f"xnq{pp}")

                        @blk.sync
                        def _(sy, ts=ts, pp=pp, xnq=xnq, xc=xn_cons2[pp]):
                            if xc > 0:
                                sy.wait_ge(b.tE.h, xc)
                            sy.dma_start(out=xnsv2[:, pp, :, :],
                                         in_=d["xn2"].rearrange("(i p) t -> p i t", p=128)[:, :, HALO + ts * 128: HALO + (ts + 1) * 128]
                                         ).then_inc(xnq.h, 16)
                        xnq.inc(16)
                    if ec > 0:
                        xnq = b.ds(f"xnq{pp}")

                        @blk.sync
                        def _(sy, ts=ts, pp=pp, xnq=xnq, xc=xn_cons2[pp]):
                            sy.wait_ge(b.tE.h, xc)
                            sy.dma_start(out=xnsv2[:, pp, :, :],
                                         in_=d["xn2"].rearrange("(i p) t -> p i t", p=128)[:, :, HALO + ts * 128: HALO + (ts + 1) * 128]
                                         ).then_inc(xnq.h, 16)
                        xnq.inc(16)
                    xq_dl = (b.ds(f"xnq{pp}").h, b.ds(f"xnq{pp}").n)

                    @blk.tensor
                    def _(t, ec=ec, ts=ts, pp=pp, v_wq=v_wq, xq_dl=xq_dl,
                          qf=q_free[pp]):
                        t.wait_ge(xq_dl[0], xq_dl[1])
                        t.wait_ge(b.vE.h, max(v_wq, qf))
                        for i in range(NDC):
                            _li = t.matmul(ps_q[pp][:, :], xnsv2[:, pp, i, :],
                                     wqTv[:, i, :], start=(i == 0), stop=(i == NDC - 1))
                        _li.then_inc(b.tE.h, 1)
                    t_q = b.tE.inc()
                    xn_cons2[pp] = t_q

                    @blk.vector
                    def _(v, ec=ec, ts=ts, pp=pp, t_q=t_q):
                        v.wait_ge(b.tE.h, t_q)
                        v.tensor_copy(qtmv[:, ts, ec * 512:(ec + 1) * 512],
                                      ps_q[pp][:, :]).then_inc(b.vE.h, 1)
                    q_free[pp] = b.vE.inc()

            v_qtm = b.vE.n
            for ts in range(8):
                @blk.vector
                def _(v, ts=ts, v_qtm=v_qtm):
                    v.wait_ge(b.vE.h, v_qtm)
                    v.wait_ge(b.tE.h, qro_cons[0])
                    gts = 4 + ts
                    qt = qtmv[:, ts, :].rearrange("p (h i two) -> p h i two", h=16, two=2)
                    qr = qro.rearrange("p (h i two) -> p h i two", h=16, two=2)
                    cc = C["cosv"][:, gts, :].unsqueeze(1).unsqueeze(3).broadcast_to((128, 16, 64, 1))
                    ss = C["sinv"][:, gts, :].unsqueeze(1).unsqueeze(3).broadcast_to((128, 16, 64, 1))
                    r0 = rsc2[:, 0:1024].rearrange("p (h i) -> p h i", h=16).unsqueeze(3)
                    r1 = rsc2[:, 1024:2048].rearrange("p (h i) -> p h i", h=16).unsqueeze(3)
                    v.tensor_tensor(r0, qt[:, :, :, 0:1], cc, op=ALU.mult)
                    li = v.tensor_tensor(r1, qt[:, :, :, 1:2], ss, op=ALU.mult)
                    b.edge(v, li, "vE")
                    li = v.tensor_tensor(qr[:, :, :, 0:1], r0, r1, op=ALU.subtract)
                    b.edge(v, li, "vE")
                    v.tensor_tensor(r0, qt[:, :, :, 0:1], ss, op=ALU.mult)
                    li = v.tensor_tensor(r1, qt[:, :, :, 1:2], cc, op=ALU.mult)
                    b.edge(v, li, "vE")
                    v.tensor_tensor(qr[:, :, :, 1:2], r0, r1, op=ALU.add).then_inc(b.vE.h, 1)
                v_qro = b.vE.inc(1)

                @blk.tensor
                def _(t, ts=ts, v_qro=v_qro):
                    t.wait_ge(b.vE.h, max(v_qro, psT3_free[0]))
                    for hc in range(16):
                        _li = t.transpose(ps_T3[:, hc * 128:(hc + 1) * 128],
                                    qro[:, hc * 128:(hc + 1) * 128],
                                    C["idn"][:, :])
                    _li.then_inc(b.tE.h, 1)
                qro_cons[0] = b.tE.inc()
                t_qtr = qro_cons[0]

                @blk.vector
                def _(v, ts=ts, t_qtr=t_qtr):
                    v.wait_ge(b.tE.h, t_qtr)
                    v.tensor_copy(qFv[:, :, ts * 128:(ts + 1) * 128],
                                  ps_T3[:, :].rearrange("p (h f) -> p h f", f=128)
                                  ).then_inc(b.vE.h, 1)
                psT3_free[0] = b.vE.inc()

            s2i.close()
            with ExitStack() as s3:
                e3 = s3.enter_context
                tb_t = e3(nc.sbuf_tensor("a_tb", [128, 2 * 512], F32))
                tb = tb_t[:, :].rearrange("p (k f) -> p k f", k=2)
                pT = e3(nc.sbuf_tensor("a_pT", [128, 8 * 512], BF16))
                dsm = e3(nc.sbuf_tensor("a_dsm", [1, 512], F32))
                rbb = e3(nc.sbuf_tensor("a_rbb", [128, 512], F32))
                ps_sT = [e3(nc.psum_tensor(f"a_pst{k}", [128, 512], F32)) for k in range(2)]
                ps_den = e3(nc.psum_tensor("a_psd", [128, 512], F32))
                ps_rb = e3(nc.psum_tensor("a_psrb", [128, 512], F32))
                ps_o = [e3(nc.psum_tensor(f"a_pso{k}", [128, 512], F32)) for k in range(2)]

                sb_masks = e3(nc.sbuf_tensor("a_masks", [128, 2 * 8 * 512], BF16))
                maskTv = sb_masks[:, :].rearrange("p (bq kt j) -> p bq kt j", bq=2, kt=8)

                mks = b.ds("mask")
                a3_bar = b.bar()

                @blk.sync
                def _(sy, mks=mks):
                    b.wait_bar(sy, a3_bar)
                    sy.dma_start(out=maskTv,
                                 in_=d["masks"].rearrange("bq m p j -> p bq m j")
                                 ).then_inc(mks.h, 16)
                    mks.inc(16)
                mask_dl = (mks.h, mks.n)

                pTv = pT[:, :].rearrange("p (j a) -> p j a", a=512)
                v_attn_in = b.vE.n

                heads = [(bq, h) for bq in range(2) for h in range(NH)]
                NHD = len(heads)
                sT_rec = {}
                pv_rec = {}
                den_rec = {}
                rb_rec = {}

                def s_h0f(hi):
                    return sA3 + 16 * hi

                def v_h0f(hi):
                    return vA3 + 11 * hi

                def emit_sT(hi, kt):
                    if hi >= NHD:
                        return
                    bq, h = heads[hi]
                    kvh = h // 4

                    @blk.tensor
                    def _(t, bq=bq, h=h, kt=kt, kvh=kvh, hi=hi):
                        if hi == 0 and kt == 0:
                            t.wait_ge(b.vE.h, v_attn_in)
                        if kt >= 2:
                            t.wait_ge(b.sE.h, s_h0f(hi) + 2 * (kt - 2) + 1)
                        elif hi > 0:
                            t.wait_ge(b.sE.h, s_h0f(hi - 1) + 2 * (kt + 6) + 1)
                        t.matmul(ps_sT[kt % 2][:, :],
                                 kFv[:, kvh, bq * 512 + kt * 128: bq * 512 + (kt + 1) * 128],
                                 qFv[:, h, bq * 512: (bq + 1) * 512],
                                 start=True, stop=True).then_inc(b.tE.h, 1)
                    sT_rec[(hi, kt)] = b.tE.inc()

                sA3 = b.sE.n
                vA3 = b.vE.n
                emit_sT(0, 0)
                emit_sT(0, 1)

                for hi in range(NHD):
                    bq, h = heads[hi]
                    kvh = h // 4
                    s_h0 = b.sE.n
                    v_h0 = b.vE.n
                    assert s_h0 == s_h0f(hi) and v_h0 == v_h0f(hi)

                    for kt in range(8):
                        if kt < 6:
                            emit_sT(hi, kt + 2)
                        else:
                            emit_sT(hi + 1, kt - 6)

                        @blk.tensor
                        def _(t, kt=kt, v_h0=v_h0):
                            t.wait_ge(b.vE.h, v_h0 + kt + 1)
                            t.matmul(ps_den[0:1, :], C["onesb"][:, :], pTv[:, kt, :],
                                     start=(kt == 0), stop=(kt == 7)).then_inc(b.tE.h, 1)
                        den_rec[(hi, kt)] = b.tE.inc()

                        @blk.tensor
                        def _(t, bq=bq, kt=kt, kvh=kvh, v_h0=v_h0, op=hi % 2):
                            t.wait_ge(b.vE.h, v_h0 + kt + 1)
                            t.matmul(ps_o[op][:, :], vv[:, bq * 4 + kt, kvh * 128:(kvh + 1) * 128],
                                     pTv[:, kt, :], start=(kt == 0),
                                     stop=(kt == 7)).then_inc(b.tE.h, 1)
                        pv_rec[(hi, kt)] = b.tE.inc()

                    @blk.tensor
                    def _(t, v_h0=v_h0):
                        t.wait_ge(b.vE.h, v_h0 + 9)
                        t.matmul(ps_rb[:, :], C["onesf"][0:1, :], dsm[0:1, :],
                                 start=True, stop=True).then_inc(b.tE.h, 1)
                    rb_rec[hi] = b.tE.inc()

                    @blk.scalar
                    def _(s, hi=hi, s_h0=s_h0):
                        for kt in range(8):
                            s.wait_ge(b.tE.h, sT_rec[(hi, kt)])
                            li = s.activation(tb[:, kt % 2, :], ps_sT[kt % 2][:, :],
                                              AF.Tanh, scale=SQ_SCALE)
                            li.then_inc(b.sE.h, 1)
                            s.wait_ge(b.sE.h, s_h0 + 2 * kt + 1)
                            s.activation(pTv[:, kt, :], tb[:, kt % 2, :], AF.Exp,
                                         scale=CAP).then_inc(b.sE.h, 1)
                    b.sE.inc(16)

                    for kt in range(8):
                        @blk.vector
                        def _(v, bq=bq, kt=kt, hi=hi, s_h0=s_h0):
                            v.wait_ge(b.sE.h, s_h0 + 2 * kt + 2)
                            if hi == 0 and kt == 0:
                                v.wait_ge(mask_dl[0], mask_dl[1])
                            v.tensor_mul(pTv[:, kt, :], pTv[:, kt, :],
                                         maskTv[:, bq, kt, :]).then_inc(b.vE.h, 1)
                        b.vE.inc()

                    @blk.vector
                    def _(v, hi=hi):
                        v.wait_ge(b.tE.h, den_rec[(hi, 7)])
                        v.reciprocal(dsm[0:1, :], ps_den[0:1, :]).then_inc(b.vE.h, 1)
                    b.vE.inc()

                    @blk.vector
                    def _(v, hi=hi):
                        v.wait_ge(b.tE.h, rb_rec[hi])
                        v.tensor_copy(rbb[:, :], ps_rb[:, :]).then_inc(b.vE.h, 1)
                    b.vE.inc()

                    @blk.vector
                    def _(v, bq=bq, h=h, hi=hi, v_h0=v_h0, op=hi % 2):
                        v.wait_ge(b.tE.h, pv_rec[(hi, 7)])
                        v.wait_ge(b.vE.h, v_h0 + 10)
                        v.tensor_mul(oFv[:, h, bq * 512:(bq + 1) * 512],
                                     ps_o[op][:, :], rbb[:, :]).then_inc(b.vE.h, 1)
                    b.vE.inc()
                    assert b.vE.n == v_h0 + 11

        with ExitStack() as s4:
            e4 = s4.enter_context
            woT = e4(nc.sbuf_tensor("a_woT", [128, 16 * 128], BF16))
            woTv = woT[:, :].rearrange("p (j f) -> p j f", f=128)
            wnat3 = e4(nc.sbuf_tensor("a_wnat3", [128, 2048], F32))
            xm2 = e4(nc.sbuf_tensor("a_xm2", [128, 1024], F32))
            stg = e4(nc.sbuf_tensor("a_stg", [128, 2 * 512], F32))
            stgv = stg[:, :].rearrange("p (k t) -> p k t", k=2)
            ps_W3 = e4(nc.psum_tensor("a_psW3", [128, 2048], F32))
            ps_y2 = [e4(nc.psum_tensor(f"a_py2{k}", [128, 512], F32)) for k in range(2)]

            psW3_free = [0]
            wnat3_free = [0]
            y2_free = [0, 0]
            stg_free = [0, 0]

            a4_bar = b.bar()
            for ds in range(NDC):
                bar = b.bar()

                wns3 = b.ds("wnat")

                @blk.sync
                def _(sy, ds=ds, wns3=wns3):
                    if ds == 0:
                        b.wait_bar(sy, a4_bar)
                    sy.wait_ge(b.tE.h, wnat3_free[0])
                    sy.dma_start(out=wnat3[:, :],
                                 in_=d["wo"].rearrange("(ds p) ee -> ds p ee", p=128)[ds]
                                 ).then_inc(wns3.h, 16)
                wns3.inc(16)
                n_dl = (wns3.h, wns3.n)

                xm2s = b.ds("xm2")

                @blk.scalar
                def _(s, ds=ds, bar=bar, xm2s=xm2s):
                    s.wait_ge(b.vE.h, bar["v"])
                    s.dma_start(out=xm2[:, :],
                                in_=d["xmid"].rearrange("(i p) t -> p i t", p=128)[:, ds, HALO:HALO + TOWN]
                                ).then_inc(xm2s.h, 16)
                xm2s.inc(16)
                x_dl = (xm2s.h, xm2s.n)

                @blk.tensor
                def _(t, ds=ds, n_dl=n_dl):
                    t.wait_ge(n_dl[0], n_dl[1])
                    t.wait_ge(b.vE.h, psW3_free[0])
                    for j in range(16):
                        _li = t.transpose(ps_W3[:, j * 128:(j + 1) * 128],
                                    wnat3[:, j * 128:(j + 1) * 128],
                                    C["idnf"][:, :])
                    _li.then_inc(b.tE.h, 1)
                t_tr = b.tE.inc()
                wnat3_free[0] = t_tr

                @blk.vector
                def _(v, t_tr=t_tr):
                    v.wait_ge(b.tE.h, t_tr)
                    v.tensor_copy(woTv[:, :, :],
                                  ps_W3[:, :].rearrange("p (j f) -> p j f", f=128)
                                  ).then_inc(b.vE.h, 1)
                psW3_free[0] = b.vE.inc()
                v_wo = psW3_free[0]

                for ah in range(2):
                    @blk.tensor
                    def _(t, ds=ds, ah=ah, v_wo=v_wo):
                        t.wait_ge(b.vE.h, max(v_wo, y2_free[ah]))
                        for j in range(16):
                            _li = t.matmul(ps_y2[ah][:, :], woTv[:, j, :],
                                     oFv[:, j, ah * 512:(ah + 1) * 512],
                                     start=(j == 0), stop=(j == 15))
                        _li.then_inc(b.tE.h, 1)
                    t_y = b.tE.inc()

                    @blk.vector
                    def _(v, ds=ds, ah=ah, t_y=t_y, x_dl=x_dl):
                        v.wait_ge(b.tE.h, t_y)
                        v.wait_ge(x_dl[0], x_dl[1])
                        if stg_free[ah] > 0:
                            v.wait_ge(b.ds(f"stg{ah}").h, stg_free[ah])
                        v.tensor_add(stgv[:, ah, :], ps_y2[ah][:, :],
                                     xm2[:, ah * 512:(ah + 1) * 512]).then_inc(b.vE.h, 1)
                    y2_free[ah] = b.vE.inc()
                    v_st = y2_free[ah]

                    fst = b.ds(f"stg{ah}")

                    @blk.scalar
                    def _(s, ds=ds, ah=ah, v_st=v_st, fst=fst):
                        s.wait_ge(b.vE.h, v_st)
                        dv = d["xatt"].rearrange("(i p) t -> p i t", p=128)
                        s.dma_start(out=dv[:, ds, ah * 512:(ah + 1) * 512],
                                    in_=stgv[:, ah, :]).then_inc(fst.h, 16)
                    fst.inc(16)
                    stg_free[ah] = fst.n


tb_free = [0]


def build_program():
    global tb_free
    tb_free = [0]
    nc = bass.Bass("TRN2", target_bir_lowering=False)
    st = ExitStack()
    d = {}
    d["x_feat"] = nc.dram_tensor("x_feat", [DM, TLOC], F32, kind="ExternalInput").ap()
    d["wq"] = nc.dram_tensor("wq", [DM, DM], F32, kind="ExternalInput").ap()
    d["wk"] = nc.dram_tensor("wk", [512, DM], F32, kind="ExternalInput").ap()
    d["wv"] = nc.dram_tensor("wv", [512, DM], F32, kind="ExternalInput").ap()
    d["wo"] = nc.dram_tensor("wo", [DM, DM], F32, kind="ExternalInput").ap()
    d["wgu"] = nc.dram_tensor("wgu", [2 * HFF, DM], F32, kind="ExternalInput").ap()
    d["wp"] = nc.dram_tensor("wp", [DM, HFF], F32, kind="ExternalInput").ap()
    d["rope_cos"] = nc.dram_tensor("rope_cos", [TLOC, 64], F32, kind="ExternalInput").ap()
    d["rope_sin"] = nc.dram_tensor("rope_sin", [TLOC, 64], F32, kind="ExternalInput").ap()
    d["masks"] = nc.dram_tensor("masks", [2, 8, 128, 512], BF16, kind="ExternalInput").ap()
    d["gsel"] = nc.dram_tensor("gsel", [16, 128, 32], BF16, kind="ExternalInput").ap()
    d["bsel"] = nc.dram_tensor("bsel", [16, 32, 128], BF16, kind="ExternalInput").ap()
    d["idn_d"] = nc.dram_tensor("idn", [128, 128], BF16, kind="ExternalInput").ap()
    d["idnf_d"] = nc.dram_tensor("idnf", [128, 128], F32, kind="ExternalInput").ap()
    d["onesb_d"] = nc.dram_tensor("onesb", [128, 1], BF16, kind="ExternalInput").ap()
    d["onesf_d"] = nc.dram_tensor("onesf", [1, 128], F32, kind="ExternalInput").ap()
    d["out_feat"] = nc.dram_tensor("out_feat", [DM, TOWN], F32, kind="ExternalOutput").ap()
    d["cache_gu"] = nc.dram_tensor("cache_gu", [64, 128, 16 * 256], BF16).ap()
    d["cache_wp"] = nc.dram_tensor("cache_wp", [16, 128, 64 * 128], BF16).ap()
    d["xmid"] = nc.dram_tensor("xmid", [DM, TLOC], F32).ap()
    d["xatt"] = nc.dram_tensor("xatt", [DM, TOWN], F32).ap()
    d["xn2"] = nc.dram_tensor("xn2", [DM, TLOC], BF16).ap()

    blk = st.enter_context(nc.Block())
    b = B(nc, st, blk)

    sb_gsel = st.enter_context(nc.sbuf_tensor("sb_gsel", [128, 16 * 32], BF16))
    sb_bsel = st.enter_context(nc.sbuf_tensor("sb_bsel", [32, 16 * 128], BF16))
    sb_idn = st.enter_context(nc.sbuf_tensor("sb_idn", [128, 128], BF16))
    sb_idnf = st.enter_context(nc.sbuf_tensor("sb_idnf", [128, 128], F32))
    sb_onesb = st.enter_context(nc.sbuf_tensor("sb_onesb", [128, 1], BF16))
    sb_onesf = st.enter_context(nc.sbuf_tensor("sb_onesf", [1, 128], F32))

    C = dict(
        gselv=sb_gsel[:, :].rearrange("p (i g) -> p i g", g=32),
        bselv=sb_bsel[:, :].rearrange("p (i m) -> p i m", m=128),
        idn=sb_idn, idnf=sb_idnf, onesb=sb_onesb, onesf=sb_onesf,
    )

    @blk.sync
    def _(sy):
        pairs = [
            (sb_gsel[:, :].rearrange("p (i g) -> p i g", g=32),
             d["gsel"].rearrange("i p g -> p i g")),
            (sb_bsel[:, :].rearrange("p (i m) -> p i m", m=128),
             d["bsel"].rearrange("i p m -> p i m")),
            (sb_idn[:, :], d["idn_d"]),
            (sb_idnf[:, :], d["idnf_d"]),
            (sb_onesb[:, :], d["onesb_d"]),
            (sb_onesf[:, :], d["onesf_d"]),
        ]
        cs = b.ds("const")
        for o, i_ in pairs:
            sy.dma_start(out=o, in_=i_).then_inc(cs.h, 16)
            cs.inc(16)
    cs = b.ds("const")

    for eng in ("tensor", "vector", "scalar", "gpsimd"):
        getattr(blk, eng)(lambda e_, h=cs.h, n=cs.n: e_.wait_ge(h, n))

    emit_quant(b, d, C)
    emit_ffn(b, d, C, which=1)
    emit_attn(b, d, C)
    emit_ffn(b, d, C, which=2)

    fin = b.bar()

    @blk.sync
    def _(sy):
        b.wait_bar(sy, fin)

    st.close()
    return nc


def kernel(L0_attn_x, L0_attn_wq, L0_attn_wk, L0_attn_wv, L0_attn_wo,
           L0_ffn_w_gate_up, L0_ffn_w_proj):
    x = np.asarray(L0_attn_x, dtype=np.float32).reshape(S, DM)
    wq = np.ascontiguousarray(np.asarray(L0_attn_wq, dtype=np.float32))
    wk = np.ascontiguousarray(np.asarray(L0_attn_wk, dtype=np.float32))
    wv = np.ascontiguousarray(np.asarray(L0_attn_wv, dtype=np.float32))
    wo = np.ascontiguousarray(np.asarray(L0_attn_wo, dtype=np.float32))
    wgu = np.ascontiguousarray(np.asarray(L0_ffn_w_gate_up, dtype=np.float32))
    wp = np.ascontiguousarray(np.asarray(L0_ffn_w_proj, dtype=np.float32))

    nc = build_program()

    in_maps = []
    for c in range(NCORES):
        consts = _host_consts(c)
        lo = c * TOWN - HALO
        xt = np.zeros((TLOC, DM), dtype=np.float32)
        v0 = max(lo, 0)
        xt[v0 - lo:, :] = x[v0:lo + TLOC, :]
        in_maps.append(dict(x_feat=np.ascontiguousarray(xt.T),
                            wq=wq, wk=wk, wv=wv, wo=wo, wgu=wgu, wp=wp, **consts))

    res = run_bass_kernel_spmd(nc, in_maps, core_ids=list(range(NCORES)))
    out = np.empty((1, S, DM), dtype=np.float32)
    for c in range(NCORES):
        out[0, c * TOWN:(c + 1) * TOWN, :] = np.asarray(res.results[c]["out_feat"]).T
    return out


# version 25
# speedup vs baseline: 1.0209x; 1.0209x over previous
import numpy as np
import ml_dtypes
from contextlib import ExitStack

import concourse.bass as bass
import concourse.mybir as mybir
from concourse.bass_utils import run_bass_kernel_spmd

F32 = mybir.dt.float32
BF16 = mybir.dt.bfloat16
AF = mybir.ActivationFunctionType
ALU = mybir.AluOpType
AX = mybir.AxisListType

NCORES = 8
S = 8192
DM = 2048
NDC = DM // 128
TOWN = 1024
HALO = 512
TLOC = TOWN + HALO
NH = 16
NKV = 4
WIN = 512
CAP = 20.0
HFF = 8192
GS = 64
INV_CTHR = float(1.0 / (0.5 + 2.0 ** -10))
TT = 512
EPS_LN = 1e-5
EPS_RMS = float(np.finfo(np.float32).eps)
SQ_SCALE = float(1.0 / (CAP * np.sqrt(128.0)))

BF = ml_dtypes.bfloat16


def _host_consts(core):
    start = core * TOWN - HALO
    pos = np.clip(np.arange(start, start + TLOC), 0, None).astype(np.float64)
    inv = 1.0 / (10000.0 ** (np.arange(0, 128, 2, dtype=np.float64) / 128.0))
    ang = pos[:, None] * inv[None, :]
    cos = np.cos(ang).astype(np.float32)
    sin = np.sin(ang).astype(np.float32)

    masks = np.zeros((2, 8, 128, 512), dtype=BF)
    j = np.arange(1024)[None, :]
    for bq in range(2):
        gblk = 2 * core + bq
        alpha = np.arange(512)[:, None]
        m = (j >= alpha) & (j <= alpha + WIN)
        if gblk == 0:
            m = m & (j >= WIN)
        masks[bq] = np.ascontiguousarray(m.T).astype(BF).reshape(8, 128, 512)

    gsel = np.zeros((16, 128, 32), dtype=BF)
    bsel = np.zeros((16, 32, 128), dtype=BF)
    for i in range(16):
        for p in range(128):
            g = ((128 * i + p) // GS) % 32
            gsel[i, p, g] = 1
            bsel[i, g, p] = 1
    return dict(rope_cos=cos, rope_sin=sin, masks=masks, gsel=gsel, bsel=bsel,
                idn=np.eye(128, dtype=BF), idnf=np.eye(128, dtype=np.float32),
                onesb=np.ones((128, 1), dtype=BF),
                onesf=np.ones((1, 128), dtype=np.float32))


class Sem:
    def __init__(self, stack, nc, name):
        self.h = stack.enter_context(nc.semaphore(name))
        self.n = 0

    def inc(self, k=1):
        self.n += k
        return self.n


class B:

    def __init__(self, nc, stack, blk):
        self.nc = nc
        self.stack = stack
        self.blk = blk
        self.dsems = {}
        self.tE = Sem(stack, nc, "tE")
        self.vE = Sem(stack, nc, "vE")
        self.sE = Sem(stack, nc, "sE")
        self.gE = Sem(stack, nc, "gE")

    def ds(self, name):
        if name not in self.dsems:
            self.dsems[name] = Sem(self.stack, self.nc, "d_" + name)
        return self.dsems[name]

    def bar(self):
        bb = dict(t=self.tE.n, v=self.vE.n, s=self.sE.n, g=self.gE.n)
        bb["dma"] = [(s.h, s.n) for s in self.dsems.values()]
        return bb

    def edge(self, eng, ins, which):
        sem = getattr(self, which)
        ins.then_inc(sem.h, 1)
        v = sem.inc()
        eng.wait_ge(sem.h, v)
        return v

    def wait_bar(self, eng, bar):
        eng.wait_ge(self.tE.h, bar["t"])
        eng.wait_ge(self.vE.h, bar["v"])
        eng.wait_ge(self.sE.h, bar["s"])
        eng.wait_ge(self.gE.h, bar["g"])
        for h, n in bar["dma"]:
            if n > 0:
                eng.wait_ge(h, n)



def emit_ln_stages(b, C, xsrc, xn_dst, T, first_bar):
    blk = b.blk
    xbf, ubuf = C["ln_xbf"], C["ln_ubuf"]
    msb, m2, e2, rssb = C["ln_msb"], C["ln_m2"], C["ln_e2"], C["ln_rssb"]
    ps_sx, ps_sxx, ps_M, ps_RS = C["ln_ps_sx"], C["ln_ps_sxx"], C["ln_ps_M"], C["ln_ps_RS"]
    xbfv = xbf.rearrange("p (k t) -> p k t", t=T)
    out = {}

    def s0():
        @blk.vector
        def _(v):
            b.wait_bar(v, first_bar)
            for i in range(NDC):
                li = v.tensor_copy(xbfv[:, i, :], xsrc(i))
            b.edge(v, li, "vE")
            for i in range(NDC):
                li = v.tensor_mul(xbfv[:, NDC + i, :], xbfv[:, i, :], xbfv[:, i, :])
            li.then_inc(b.vE.h, 1)
        out["v_sq"] = b.vE.inc()

    def s1():
        @blk.tensor
        def _(t, v_sq=out["v_sq"]):
            t.wait_ge(b.vE.h, v_sq)
            for i in range(NDC):
                _li = t.matmul(ps_sx[0:1, 0:T], C["onesb"][:, :], xbfv[:, i, :],
                         start=(i == 0), stop=(i == NDC - 1))
            for i in range(NDC):
                _li = t.matmul(ps_sxx[0:1, 0:T], C["onesb"][:, :], xbfv[:, NDC + i, :],
                         start=(i == 0), stop=(i == NDC - 1))
            _li.then_inc(b.tE.h, 1)
        out["t_stats"] = b.tE.inc()

    def s2():
        @blk.vector
        def _(v, t_stats=out["t_stats"]):
            v.wait_ge(b.tE.h, t_stats)
            li = v.tensor_scalar(msb[0:1, 0:T], ps_sx[0:1, 0:T], 1.0 / DM, None, op0=ALU.mult)
            b.edge(v, li, "vE")
            li = v.tensor_mul(m2[0:1, 0:T], msb[0:1, 0:T], msb[0:1, 0:T])
            b.edge(v, li, "vE")
            li = v.tensor_scalar(e2[0:1, 0:T], ps_sxx[0:1, 0:T], 1.0 / DM, EPS_LN,
                                 op0=ALU.mult, op1=ALU.add)
            b.edge(v, li, "vE")
            v.tensor_sub(e2[0:1, 0:T], e2[0:1, 0:T], m2[0:1, 0:T]).then_inc(b.vE.h, 1)
        v_var = b.vE.inc()

        @blk.scalar
        def _(s, v_var=v_var):
            s.wait_ge(b.vE.h, v_var)
            li = s.activation(e2[0:1, 0:T], e2[0:1, 0:T], AF.Ln)
            b.edge(s, li, "sE")
            s.activation(rssb[0:1, 0:T], e2[0:1, 0:T], AF.Exp, scale=-0.5).then_inc(b.sE.h, 1)
        out["s_rs"] = b.sE.inc()

    def s3():
        @blk.tensor
        def _(t, s_rs=out["s_rs"]):
            t.wait_ge(b.sE.h, s_rs)
            _li = t.matmul(ps_M[:, 0:T], C["onesf"][0:1, :], msb[0:1, 0:T], start=True, stop=True)
            _li = t.matmul(ps_RS[:, 0:T], C["onesf"][0:1, :], rssb[0:1, 0:T], start=True, stop=True)
            _li.then_inc(b.tE.h, 1)
        out["t_bc"] = b.tE.inc()

    def s4():
        @blk.vector
        def _(v, t_bc=out["t_bc"]):
            v.wait_ge(b.tE.h, t_bc)
            for i in range(NDC):
                li = v.scalar_tensor_tensor(ubuf[:, 0:T], xsrc(i), 1.0, ps_M[:, 0:T],
                                            op0=ALU.mult, op1=ALU.subtract)
                b.edge(v, li, "vE")
                li = v.tensor_mul(xn_dst(i), ubuf[:, 0:T], ps_RS[:, 0:T])
                if i < NDC - 1:
                    b.edge(v, li, "vE")
            li.then_inc(b.vE.h, 1)
        out["v_ln"] = b.vE.inc()

    return [s0, s1, s2, s3, s4], out


def emit_ln(b, C, xsrc, xn_dst, T, first_bar):
    stages, out = emit_ln_stages(b, C, xsrc, xn_dst, T, first_bar)
    for st in stages:
        st()
    return out["v_ln"]



def emit_quant(b, d, C):
    blk = b.blk
    nc = b.nc
    with ExitStack() as qs:
        e = qs.enter_context
        nat = e(nc.sbuf_tensor("q_nat", [128, 4 * 2048], F32))
        natb = e(nc.sbuf_tensor("q_natb", [128, 2 * 4 * 2048], BF16))
        aT = e(nc.sbuf_tensor("q_aT", [128, 2 * 16 * 512], BF16))
        sig = e(nc.sbuf_tensor("q_sig", [128, 2 * 16 * 512], BF16))
        sg = e(nc.sbuf_tensor("q_sg", [32, 2 * 512], BF16))
        thf = e(nc.sbuf_tensor("q_thf", [32, 2 * 512], F32))
        thb = e(nc.sbuf_tensor("q_thb", [32, 2 * 512], BF16))
        mm_t = e(nc.sbuf_tensor("q_mm", [128, 2 * 512], BF16))
        tau_t = e(nc.sbuf_tensor("q_tau", [128, 2 * 512], BF16))
        whb = e(nc.sbuf_tensor("q_whb", [128, 16 * 512], BF16))
        ps_T = [e(nc.psum_tensor(f"q_psT{k}", [128, 1024], BF16)) for k in range(3)]
        ps_G = e(nc.psum_tensor("q_psG", [32, 512], F32))
        ps_S = [e(nc.psum_tensor(f"q_psS{k}", [128, 512], F32)) for k in range(2)]
        ps_TH = [e(nc.psum_tensor(f"q_psTH{k}", [128, 512], F32)) for k in range(2)]

        natv = nat[:, :].rearrange("p (r f) -> p r f", r=4)
        natbv = natb[:, :].rearrange("p (sp r f) -> p sp r f", sp=2, r=4)
        aTv_ = aT[:, :].rearrange("p (sp i f) -> p sp i f", sp=2, f=512)
        sigv_ = sig[:, :].rearrange("p (sp i f) -> p sp i f", sp=2, f=512)
        sgv = sg[:, :].rearrange("p (sp f) -> p sp f", sp=2)
        thfv = thf[:, :].rearrange("p (sp f) -> p sp f", sp=2)
        thbv = thb[:, :].rearrange("p (sp f) -> p sp f", sp=2)
        mmv = mm_t[:, :].rearrange("p (k f) -> p k f", k=2)
        tauv = tau_t[:, :].rearrange("p (k f) -> p k f", k=2)

        K_ITER = None

        class SetCtx:
            pass

        def emit_pipeline(sets):
            nsets = len(sets)
            prevS0 = 0
            sig_freeG = [[0] * 16, [0] * 16]
            aT_freeV = [0, 0]
            aT_freeT = [0, 0]
            whb_free = 0
            psS_carry = [0, 0]
            gsum_last = 0
            sg_read = 0
            slot_free = [0, 0]
            nat_dls = {}
            cast_vc = {}
            cast_sc = {}
            prev_emb = False

            def emit_load(kk):
                if kk >= nsets:
                    return
                qn = b.ds(f"qnat{kk % 2}")
                cr = (cast_vc.get(kk - 1, 0), cast_sc.get(kk - 1, 0))

                @blk.sync
                def _(sy, cur=sets[kk], qn=qn, cr=cr):
                    if cr[0] > 0:
                        sy.wait_ge(b.vE.h, cr[0])
                    if cr[1] > 0:
                        sy.wait_ge(b.sE.h, cr[1])
                    sy.dma_start(out=natv[:, :, :],
                                 in_=cur.load_ap).then_inc(qn.h, 16)
                qn.inc(16)
                nat_dls[kk] = (qn.h, qn.n)

            def emit_vcasts(kk):
                if kk >= nsets:
                    return
                sl = kk % 2
                nat_dl = nat_dls[kk]

                @blk.vector
                def _(v, sl=sl, nat_dl=nat_dl, sf=slot_free[sl]):
                    v.wait_ge(nat_dl[0], nat_dl[1])
                    if sf > 0:
                        v.wait_ge(b.tE.h, sf)
                    v.tensor_copy(natbv[:, sl, 0, :], natv[:, 0, :]).then_inc(b.vE.h, 1)
                    v.tensor_copy(natbv[:, sl, 1, :], natv[:, 1, :]).then_inc(b.vE.h, 1)
                b.vE.inc(2)
                cast_vc[kk] = b.vE.n

            emit_load(0)
            emit_vcasts(0)

            @blk.scalar
            def _(s, nat_dl=nat_dls[0]):
                s.wait_ge(nat_dl[0], nat_dl[1])
                s.activation(natbv[:, 0, 2, :], natv[:, 2, :],
                             AF.Copy).then_inc(b.sE.h, 1)
                s.activation(natbv[:, 0, 3, :], natv[:, 3, :],
                             AF.Copy).then_inc(b.sE.h, 1)
            b.sE.inc(2)
            cast_sc[0] = b.sE.n
            for k in range(nsets + 1):
                cur = sets[k] if k < nsets else None
                prv = sets[k - 1] if k >= 1 else None
                T0, S0, V0, G0 = b.tE.n, b.sE.n, b.vE.n, b.gE.n
                sp = k % 2
                pv = (k - 1) % 2
                gsum_prev = gsum_last
                emit_load(k + 1)


                emb_k = (cur is not None) and (k + 1 < nsets)

                def s_abs(S0_, pk, emb):
                    return S0_ + 2 * pk + 1 + (2 if (emb and pk >= 4) else 0)

                def s_sign(S0_, pk, emb):
                    return s_abs(S0_, pk, emb) + 1

                v_base = V0 + (4 if prv is not None else 0)

                def v_m(i):
                    return v_base + 3 * i + 1

                def v_tau(i):
                    return v_base + 3 * i + 2

                def v_wh(i):
                    return v_base + 3 * i + 3

                tr_rec = {}
                bc_rec = {}
                gs_rec = {}
                sl = k % 2

                def emit_tr(pk):
                    @blk.tensor
                    def _(t, pk=pk, sl=sl, vck=cast_vc[k], sck=cast_sc[k],
                          pS0=prevS0, S0=S0, emb=emb_k, pemb=prev_emb):
                        if pk == 0:
                            t.wait_ge(b.vE.h, vck)
                            t.wait_ge(b.sE.h, sck)
                        if pk >= 3:
                            t.wait_ge(b.sE.h, s_sign(S0, pk - 3, emb))
                        elif pS0 > 0:
                            t.wait_ge(b.sE.h,
                                      s_sign(pS0, {0: 6, 1: 7, 2: 5}[pk], pemb))
                        for ii in (2 * pk, 2 * pk + 1):
                            for r in range(4):
                                _li = t.transpose(
                                    ps_T[pk % 3][:, (ii % 2) * 512 + r * 128:
                                                 (ii % 2) * 512 + (r + 1) * 128],
                                    natbv[:, sl, r, ii * 128:(ii + 1) * 128],
                                    C["idn"][:, :])
                        _li.then_inc(b.tE.h, 1)
                    tr_rec[pk] = b.tE.inc()

                def emit_gsum(i):
                    @blk.tensor
                    def _(t, i=i, sp=sp, sgr=sg_read, S0=S0, emb=emb_k,
                          aTv=aTv_[:, sp]):
                        t.wait_ge(b.sE.h, s_sign(S0, 7, emb) if i == 15
                                  else s_abs(S0, i // 2, emb))
                        if i == 0:
                            if sgr > 0:
                                t.wait_ge(b.vE.h, sgr)
                        t.matmul(ps_G[:, :], C["gselv"][:, i, :], aTv[:, i, :],
                                 start=(i == 0), stop=(i == 15)).then_inc(b.tE.h, 1)
                    gs_rec[i] = b.tE.inc()

                def emit_bc(i):
                    pp = i % 2

                    @blk.tensor
                    def _(t, i=i, pp=pp, V0=V0, pv=pv, psc=psS_carry[i % 2],
                          v_base=v_base):
                        t.wait_ge(b.vE.h, V0 + 4)
                        if i >= 2:
                            t.wait_ge(b.vE.h, v_wh(i - 2))
                        elif psc > 0:
                            t.wait_ge(b.vE.h, psc)
                        t.matmul(ps_S[pp][:, :], C["bselv"][:, i, :], sgv[:, pv, :],
                                 start=True, stop=True)
                        t.matmul(ps_TH[pp][:, :], C["bselv"][:, i, :], thbv[:, pv, :],
                                 start=True, stop=True).then_inc(b.tE.h, 1)
                    bc_rec[i] = b.tE.inc()

                if cur is not None and prv is not None:
                    for pk in range(8):
                        emit_tr(pk)
                        for j in (2 * pk - 2, 2 * pk - 1):
                            if 0 <= j < 16:
                                emit_bc(j)
                        for j in (2 * pk - 4, 2 * pk - 3):
                            if 0 <= j < 16:
                                emit_gsum(j)
                    emit_bc(14)
                    emit_bc(15)
                    for i in (12, 13, 14, 15):
                        emit_gsum(i)
                elif cur is not None:
                    for pk in range(8):
                        emit_tr(pk)
                        for j in (2 * pk - 4, 2 * pk - 3):
                            if 0 <= j < 16:
                                emit_gsum(j)
                    for i in (12, 13, 14, 15):
                        emit_gsum(i)
                elif prv is not None:
                    for i in range(16):
                        emit_bc(i)
                if cur is not None:
                    gsum_last = gs_rec[15]
                    slot_free[sl] = tr_rec[7]

                if cur is not None:
                    @blk.scalar
                    def _(s, S0=S0, sp=sp, kk=k, atV=aT_freeV[sp], emb=emb_k,
                          trp=[tr_rec[pk] for pk in range(8)],
                          sgf=list(sig_freeG[sp]), sigv=sigv_[:, sp],
                          aTv=aTv_[:, sp]):
                        for pk in range(8):
                            s.wait_ge(b.tE.h, trp[pk])
                            if pk == 0 and atV > 0:
                                s.wait_ge(b.vE.h, atV)
                            li = s.activation(aTv[:, 2 * pk:2 * pk + 2, :],
                                              ps_T[pk % 3][:, :].rearrange(
                                                  "p (a f) -> p a f", a=2), AF.Abs)
                            li.then_inc(b.sE.h, 1)
                            if sgf[2 * pk + 1] > 0:
                                s.wait_ge(b.vE.h, sgf[2 * pk + 1])
                            s.activation(sigv[:, 2 * pk:2 * pk + 2, :],
                                         ps_T[pk % 3][:, :].rearrange(
                                             "p (a f) -> p a f", a=2),
                                         AF.Sign).then_inc(b.sE.h, 1)
                            if pk == 3 and emb:
                                sl2 = (kk + 1) % 2
                                nd = nat_dls[kk + 1]
                                s.wait_ge(nd[0], nd[1])
                                s.activation(natbv[:, sl2, 2, :], natv[:, 2, :],
                                             AF.Copy).then_inc(b.sE.h, 1)
                                s.activation(natbv[:, sl2, 3, :], natv[:, 3, :],
                                             AF.Copy).then_inc(b.sE.h, 1)
                    b.sE.inc(16 + (2 if emb_k else 0))
                    if emb_k:
                        cast_sc[k + 1] = S0 + 10
                    assert b.sE.n == s_sign(S0, 7, emb_k)

                if prv is not None:
                    @blk.vector
                    def _(v, gsl=gsum_prev, pv=pv):
                        v.wait_ge(b.tE.h, gsl)
                        li = v.tensor_scalar(sgv[:, pv, :], ps_G[:, :], 1.0 / GS, 1e-8,
                                             op0=ALU.mult, op1=ALU.max)
                        b.edge(v, li, "vE")
                        li = v.tensor_scalar(thfv[:, pv, :], sgv[:, pv, :],
                                             0.5 + 2.0 ** -10, None, op0=ALU.mult)
                        b.edge(v, li, "vE")
                        li = v.tensor_single_scalar(thfv[:, pv, :].bitcast(mybir.dt.int32),
                                                    thfv[:, pv, :].bitcast(mybir.dt.int32),
                                                    -65536, op=ALU.bitwise_and)
                        b.edge(v, li, "vE")
                        v.tensor_copy(thbv[:, pv, :], thfv[:, pv, :]).then_inc(b.vE.h, 1)
                    b.vE.inc()
                    sg_read = V0 + 4
                    assert b.vE.n == v_base

                    for i in range(16):
                        pp = i % 2

                        @blk.vector
                        def _(v, i=i, pp=pp, pv=pv, pS0=prevS0, pemb=prev_emb,
                              whf=whb_free, prv=prv, aTv=aTv_[:, pv],
                              sigv=sigv_[:, pv], bcr=bc_rec[i]):
                            v.wait_ge(b.tE.h, bcr)
                            v.tensor_tensor(mmv[:, pp, :], aTv[:, i, :],
                                            ps_TH[pp][:, :],
                                            op=ALU.is_gt).then_inc(b.vE.h, 1)
                            v.wait_ge(b.vE.h, v_m(i))
                            v.wait_ge(b.sE.h, s_sign(pS0, i // 2, pemb))
                            v.tensor_mul(tauv[:, pp, :], mmv[:, pp, :],
                                         sigv[:, i, :]).then_inc(b.vE.h, 1)
                            v.wait_ge(b.vE.h, v_tau(i))
                            if i == 0 and whf > 0:
                                v.wait_ge(b.ds("qwh").h, whf)
                            v.tensor_mul(prv.wh_out(i),
                                         tauv[:, pp, :].rearrange("p (a f) -> p a f", a=prv.wh_out_a),
                                         ps_S[pp][:, :].rearrange("p (a f) -> p a f", a=prv.wh_out_a)
                                         ).then_inc(b.vE.h, 1)
                        b.vE.inc(3)
                        assert b.vE.n == v_wh(i)
                        sig_freeG[pv][i] = v_tau(i)
                    psS_carry = [v_wh(14), v_wh(15)]
                    aT_freeV[pv] = v_m(15)
                    emit_vcasts(k + 1)
                elif cur is not None:
                    emit_vcasts(k + 1)

                if cur is not None:
                    aT_freeT[sp] = gsum_last
                if prv is not None:
                    v_whb = v_wh(15)

                    @blk.sync
                    def _(sy, prv=prv, v_whb=v_whb):
                        sy.wait_ge(b.vE.h, v_whb)
                        prv.cache_write(sy)
                    whb_free = b.ds("qwh").n
                if cur is not None:
                    prevS0 = S0
                    prev_emb = emb_k


        sets = []
        wgu_re = d["wgu"].rearrange("(c r p) dd -> c p r dd", r=4, p=128)
        whb_gu = whb[:, :].rearrange("p (cc i f) -> p cc i f", cc=2, i=16)
        for c in range(32):
            sc = SetCtx()
            sc.load_ap = wgu_re[c]
            sc.wh_out_a = 2
            sc.sp = len(sets) % 2
            sc.wh_out = (lambda i: whb_gu[:, :, i, :])

            def cw_gu(sy, c=c):
                for cc in range(2):
                    sy.dma_start(out=d["cache_gu"][2 * c + cc],
                                 in_=whb_gu[:, cc, :, :]).then_inc(b.ds("qwh").h, 16)
                    b.ds("qwh").inc(16)
            sc.cache_write = cw_gu
            sets.append(sc)

        wp_re = d["wp"].rearrange("(c r p) hh -> c p r hh", r=4, p=128)
        whb_wp = whb[:, :].rearrange("p (ds jj f) -> p ds jj f", ds=4, jj=16)
        cwp = d["cache_wp"].rearrange("ds p (j f) -> ds p j f", f=128)
        for c in range(4):
            for s_ in range(4):
                sc = SetCtx()
                sc.load_ap = wp_re[c][:, :, 2048 * s_: 2048 * (s_ + 1)]
                sc.wh_out_a = 4
                sc.sp = len(sets) % 2
                sc.wh_out = (lambda jj: whb_wp[:, :, jj, :])

                def cw_wp(sy, c=c, s_=s_):
                    for ds in range(4):
                        sy.dma_start(out=cwp[4 * c + ds][:, 16 * s_:16 * s_ + 16, :],
                                     in_=whb_wp[:, ds, :, :]).then_inc(b.ds("qwh").h, 16)
                        b.ds("qwh").inc(16)
                sc.cache_write = cw_wp
                sets.append(sc)

        emit_pipeline(sets)


def emit_ffn(b, d, C, which):
    blk = b.blk
    nc = b.nc
    ntiles = 3 if which == 1 else 2
    src_dram = d["x_feat"] if which == 1 else d["xatt"]
    dst_dram = d["xmid"] if which == 1 else d["out_feat"]
    pref = f"f{which}_"
    with ExitStack() as fs:
        e = fs.enter_context
        xtile = e(nc.sbuf_tensor(pref + "xt", [128, NDC * TT], F32))
        xn = e(nc.sbuf_tensor(pref + "xn", [128, NDC * TT], BF16))
        hbuf = e(nc.sbuf_tensor(pref + "h", [128, 64 * TT], BF16))
        wgub = e(nc.sbuf_tensor(pref + "wgu", [128, 2 * 2 * 16 * 256], BF16))
        wpb = e(nc.sbuf_tensor(pref + "wp", [128, 2 * 64 * 128], BF16))
        obuf = e(nc.sbuf_tensor(pref + "ob", [128, 2 * TT], F32))

        xnv = xn[:, :].rearrange("p (i t) -> p i t", t=TT)
        hv = hbuf[:, :].rearrange("p (j t) -> p j t", t=TT)
        wguv = wgub[:, :].rearrange("p (sl gu i f) -> p sl gu i f", sl=2, gu=2, i=16)
        wpv = wpb[:, :].rearrange("p (sl j f) -> p sl j f", sl=2, j=64)
        obv = obuf[:, :].rearrange("p (k t) -> p k t", k=2)

        wgu_free = [0, 0]
        wp_free = [0, 0]
        gu_free = [0, 0, 0, 0]
        gb_free = [0, 0]
        y_free = [0, 0, 0, 0]
        ob_free = [0, 0]
        h2_cons = [0, 0]
        xres_free = [0, 0, 0]
        phase_bar = b.bar()
        ln_carry = {}

        def xsrc(i):
            return xtile[:, :].rearrange("p (i t) -> p i t", t=TT)[:, i, :]

        def setup_ln_bufs(stk, tag, h2buf=None):
            el = stk.enter_context
            C["ln_xbf"] = wgub[:, 0:2 * NDC * TT]
            if h2buf is None:
                C["ln_ubuf"] = el(nc.sbuf_tensor(pref + f"lu{tag}", [128, TT], F32))
                sm = el(nc.sbuf_tensor(pref + f"lm{tag}", [1, 4 * TT], F32))
                C["ln_msb"] = sm[0:1, 0 * TT:1 * TT]
                C["ln_rssb"] = sm[0:1, 1 * TT:2 * TT]
                C["ln_m2"] = sm[0:1, 2 * TT:3 * TT]
                C["ln_e2"] = sm[0:1, 3 * TT:4 * TT]
            else:
                h2b, mssb = h2buf
                C["ln_ubuf"] = h2b[:, 0:2 * TT].bitcast(F32)
                C["ln_msb"] = h2b[0:1, 2 * TT:4 * TT].bitcast(F32)
                C["ln_m2"] = h2b[0:1, 4 * TT:6 * TT].bitcast(F32)
                C["ln_e2"] = h2b[0:1, 6 * TT:8 * TT].bitcast(F32)
                C["ln_rssb"] = mssb[0:1, 0:TT]
            C["ln_ps_sx"] = el(nc.psum_tensor(pref + f"lpx{tag}", [128, TT], F32))
            C["ln_ps_sxx"] = el(nc.psum_tensor(pref + f"lpy{tag}", [128, TT], F32))
            C["ln_ps_M"] = el(nc.psum_tensor(pref + f"lpm{tag}", [128, TT], F32))
            C["ln_ps_RS"] = el(nc.psum_tensor(pref + f"lpr{tag}", [128, TT], F32))

        for ti in range(ntiles):
            t0 = ti * TT

            if ti == 0:
                fxt = b.ds(pref + "xt")

                @blk.sync
                def _(sy, t0=t0, fxt=fxt):
                    b.wait_bar(sy, phase_bar)
                    sy.dma_start(out=xtile[:, :].rearrange("p (i t) -> p i t", t=TT),
                                 in_=src_dram.rearrange("(i p) t -> p i t", p=128)[:, :, t0:t0 + TT]
                                 ).then_inc(fxt.h, 16)
                fxt.inc(16)

                with ExitStack() as lns:
                    setup_ln_bufs(lns, f"{ti}")
                    stages, lnout = emit_ln_stages(b, C, xsrc,
                                                   lambda i: xnv[:, i, :], TT, b.bar())
                    for st in stages:
                        st()
                v_ln = lnout["v_ln"]
                t_stats = lnout["t_stats"]
            else:
                v_ln = ln_carry[ti]["v_ln"]
                t_stats = ln_carry[ti]["t_stats"]

            with ExitStack() as gs_:
                eg = gs_.enter_context
                ps_g = [eg(nc.psum_tensor(pref + f"pg{ti}_{k}", [128, TT], F32)) for k in range(4)]
                ps_u = [eg(nc.psum_tensor(pref + f"pu{ti}_{k}", [128, TT], F32)) for k in range(4)]
                gbuf = eg(nc.sbuf_tensor(pref + f"g{ti}", [128, 4 * TT], BF16))
                ubuf2 = eg(nc.sbuf_tensor(pref + f"u2{ti}", [128, 2 * TT], BF16))
                gbv = gbuf[:, :].rearrange("p (k t) -> p k t", k=4)
                ubv = ubuf2[:, :].rearrange("p (k t) -> p k t", k=2)

                tA = b.tE.n
                vA = b.vE.n
                sA = b.sE.n

                def t_gu_f(hc):
                    return tA + hc + 1

                def v_h_f(j):
                    return vA + 2 * (j + 1)

                def s_g_f(j):
                    return sA + j + 1

                @blk.scalar
                def _(s, tA=tA, vA=vA, gbA=list(gb_free)):
                    for hc in range(32):
                        for hs in range(2):
                            j = 2 * hc + hs
                            s.wait_ge(b.tE.h, t_gu_f(hc))
                            if j >= 4:
                                s.wait_ge(b.vE.h, v_h_f(j - 4))
                            elif gbA[j % 2] > 0:
                                s.wait_ge(b.vE.h, gbA[j % 2])
                            s.activation(gbv[:, j % 4, :], ps_g[2 * (hc % 2) + hs][:, :],
                                         AF.Silu).then_inc(b.sE.h, 1)
                b.sE.inc(64)
                assert b.sE.n == s_g_f(63)

                for hc in range(32):
                    par = hc % 2
                    sl = par

                    fwg = b.ds(pref + f"wgu{sl}")

                    @blk.sync
                    def _(sy, hc=hc, sl=sl, ti=ti, fwg=fwg,
                          wf=max(wgu_free[hc % 2], t_stats if hc < 2 else 0)):
                        if ti == 0 and hc == 0:
                            b.wait_bar(sy, phase_bar)
                        sy.wait_ge(b.tE.h, wf)
                        sy.dma_start(out=wguv[:, sl, 0, :, :],
                                     in_=d["cache_gu"][hc].rearrange("p (i f) -> p i f", f=256)
                                     ).then_inc(fwg.h, 16)
                        fwg.inc(16)
                        sy.dma_start(out=wguv[:, sl, 1, :, :],
                                     in_=d["cache_gu"][32 + hc].rearrange("p (i f) -> p i f", f=256)
                                     ).then_inc(fwg.h, 16)
                        fwg.inc(16)
                    w_dl = (fwg.h, fwg.n)

                    @blk.tensor
                    def _(t, sl=sl, par=par, w_dl=w_dl, v_ln=v_ln, hc=hc):
                        t.wait_ge(w_dl[0], w_dl[1])
                        vw = max(v_ln, gu_free[2 * par], gu_free[2 * par + 1])
                        t.wait_ge(b.vE.h, vw)
                        for gu in range(2):
                            for hs in range(2):
                                ps = (ps_g if gu == 0 else ps_u)[2 * par + hs]
                                for i in range(NDC):
                                    _li = t.matmul(ps[:, :],
                                             wguv[:, sl, gu, i, hs * 128:(hs + 1) * 128],
                                             xnv[:, i, :],
                                             start=(i == 0), stop=(i == NDC - 1))
                        _li.then_inc(b.tE.h, 1)
                    wgu_free[sl] = b.tE.inc()
                    t_gu = wgu_free[sl]
                    assert t_gu == t_gu_f(hc)

                    for hs in range(2):
                        j = 2 * hc + hs

                        @blk.vector
                        def _(v, par=par, hs=hs, j=j, t_gu=t_gu):
                            v.wait_ge(b.tE.h, t_gu)
                            li = v.tensor_copy(ubv[:, hs, :], ps_u[2 * par + hs][:, :])
                            b.edge(v, li, "vE")
                            v.wait_ge(b.sE.h, s_g_f(j))
                            v.tensor_mul(hv[:, j, :], gbv[:, j % 4, :],
                                         ubv[:, hs, :]).then_inc(b.vE.h, 1)
                        gu_free[2 * par + hs] = b.vE.inc()
                        assert gu_free[2 * par + hs] == v_h_f(j)
                gb_free[0] = b.vE.n
                gb_free[1] = b.vE.n

            with ExitStack() as ps_:
                ep = ps_.enter_context
                ps_ss = ep(nc.psum_tensor(pref + f"pss{ti}", [128, TT], F32))
                ps_y = [ep(nc.psum_tensor(pref + f"py{ti}_{k}", [128, TT], F32)) for k in range(3)]
                h2buf = ep(nc.sbuf_tensor(pref + f"h2{ti}", [128, 8 * TT], BF16))
                mss = ep(nc.sbuf_tensor(pref + f"mss{ti}", [1, TT], F32))
                rssb_b = ep(nc.sbuf_tensor(pref + f"rsb{ti}", [128, TT], F32))
                xres = ep(nc.sbuf_tensor(pref + f"xr{ti}", [128, 3 * TT], F32))
                h2v = h2buf[:, :].rearrange("p (k t) -> p k t", k=8)
                xresv = xres[:, :].rearrange("p (k t) -> p k t", k=3)

                v_h_all = b.vE.n
                v_h2s = {}
                xres_dls = {}

                def emit_h2(qq):
                    if qq >= 16:
                        return

                    @blk.vector
                    def _(v, qq=qq, v_h_all=v_h_all, hc=h2_cons[qq % 2]):
                        v.wait_ge(b.vE.h, v_h_all)
                        if hc > 0:
                            v.wait_ge(b.tE.h, hc)
                        for jj in range(4):
                            li = v.tensor_mul(h2v[:, (qq % 2) * 4 + jj, :],
                                              hv[:, 4 * qq + jj, :],
                                              hv[:, 4 * qq + jj, :])
                        li.then_inc(b.vE.h, 1)
                    v_h2s[qq] = b.vE.inc()

                def emit_xres(ds):
                    if ds >= NDC:
                        return
                    sl4 = ds % 3
                    fxr = b.ds(pref + f"xr{sl4}")

                    @blk.sync
                    def _(sy, ds=ds, sl4=sl4, fxr=fxr, wf=xres_free[sl4], t0=t0):
                        if wf > 0:
                            sy.wait_ge(b.vE.h, wf)
                        sy.dma_start(out=xresv[:, sl4, :],
                                     in_=src_dram.rearrange("(i p) t -> p i t", p=128)[:, ds, t0:t0 + TT]
                                     ).then_inc(fxr.h, 16)
                    fxr.inc(16)
                    xres_dls[ds] = (fxr.h, fxr.n)

                wp_dls = {}

                def emit_wp_load(ds):
                    if ds >= NDC:
                        return
                    sl = ds % 2
                    fwp = b.ds(pref + f"wp{sl}")
                    fwp2 = b.ds(pref + f"wp{sl}b")
                    wf = wp_free[sl]
                    src = d["cache_wp"][ds].rearrange("p (j f) -> p j f", f=128)

                    @blk.sync
                    def _(sy, sl=sl, wf=wf, fwp=fwp, src=src):
                        sy.wait_ge(b.tE.h, wf)
                        sy.dma_start(out=wpv[:, sl, 0:32, :],
                                     in_=src[:, 0:32, :]).then_inc(fwp.h, 16)
                    fwp.inc(16)
                    if ds < 2:
                        @blk.sync
                        def _(sy, sl=sl, wf=wf, fwp2=fwp2, src=src):
                            sy.dma_start(out=wpv[:, sl, 32:64, :],
                                         in_=src[:, 32:64, :]).then_inc(fwp2.h, 16)
                        fwp2.inc(16)
                    else:
                        @blk.scalar
                        def _(s, sl=sl, wf=wf, fwp2=fwp2, src=src):
                            s.wait_ge(b.tE.h, wf)
                            s.dma_start(out=wpv[:, sl, 32:64, :],
                                        in_=src[:, 32:64, :]).then_inc(fwp2.h, 16)
                        fwp2.inc(16)
                    wp_dls[ds] = ((fwp.h, fwp.n), (fwp2.h, fwp2.n))

                emit_wp_load(0)
                emit_wp_load(1)
                emit_xres(0)
                emit_xres(1)
                emit_xres(2)
                emit_h2(0)
                emit_h2(1)

                def emit_ss(qq):
                    @blk.tensor
                    def _(t, qq=qq, v_h2=v_h2s[qq]):
                        t.wait_ge(b.vE.h, v_h2)
                        for jj in range(4):
                            _li = t.matmul(ps_ss[0:1, 0:TT], C["onesb"][:, :],
                                           h2v[:, (qq % 2) * 4 + jj, :],
                                           start=(qq == 0 and jj == 0),
                                           stop=(qq == 15 and jj == 3))
                        _li.then_inc(b.tE.h, 1)
                    h2_cons[qq % 2] = b.tE.inc()

                def emit_rs():
                    @blk.vector
                    def _(v, t_ss=h2_cons[15 % 2]):
                        v.wait_ge(b.tE.h, t_ss)
                        v.tensor_scalar(mss[0:1, 0:TT], ps_ss[0:1, 0:TT], 1.0 / HFF, EPS_RMS,
                                        op0=ALU.mult, op1=ALU.add).then_inc(b.vE.h, 1)
                    v_ms = b.vE.inc()

                    @blk.scalar
                    def _(s, v_ms=v_ms):
                        s.wait_ge(b.vE.h, v_ms)
                        li = s.activation(mss[0:1, 0:TT], mss[0:1, 0:TT], AF.Ln)
                        b.edge(s, li, "sE")
                        s.activation(mss[0:1, 0:TT], mss[0:1, 0:TT], AF.Exp,
                                     scale=-0.5).then_inc(b.sE.h, 1)
                    s_rs = b.sE.inc()

                    @blk.tensor
                    def _(t, s_rs=s_rs):
                        t.wait_ge(b.sE.h, s_rs)
                        _li = t.matmul(ps_ss[:, 0:TT], C["onesf"][0:1, :], mss[0:1, 0:TT],
                                 start=True, stop=True).then_inc(b.tE.h, 1)
                    t_rb = b.tE.inc()

                    @blk.vector
                    def _(v, t_rb=t_rb):
                        v.wait_ge(b.tE.h, t_rb)
                        v.tensor_copy(rssb_b[:, 0:TT], ps_ss[:, 0:TT]).then_inc(b.vE.h, 1)
                    return b.vE.inc()

                t_ys = {}
                v_rsb = None

                def emit_epi(ds, v_rsb_):
                    par = ds % 3
                    obp = ds % 2

                    @blk.vector
                    def _(v, ds=ds, par=par, obp=obp, t_y=t_ys[ds], v_rsb_=v_rsb_,
                          xr_dl=xres_dls[ds], obf=ob_free[obp]):
                        v.wait_ge(b.tE.h, t_y)
                        if v_rsb_ is not None:
                            v.wait_ge(b.vE.h, v_rsb_)
                        v.wait_ge(xr_dl[0], xr_dl[1])
                        if obf > 0:
                            v.wait_ge(b.ds(pref + f"ob{obp}").h, obf)
                        li = v.scalar_tensor_tensor(obv[:, obp, :], ps_y[par][:, 0:TT], 0.5,
                                                    rssb_b[:, 0:TT], op0=ALU.mult, op1=ALU.mult)
                        b.edge(v, li, "vE")
                        v.tensor_add(obv[:, obp, :], obv[:, obp, :],
                                     xresv[:, ds % 3, :]).then_inc(b.vE.h, 1)
                    y_free[par] = b.vE.inc()
                    xres_free[ds % 3] = y_free[par]
                    emit_xres(ds + 3)
                    v_o = y_free[par]

                    fob = b.ds(pref + f"ob{obp}")

                    @blk.sync
                    def _(sy, ds=ds, obp=obp, v_o=v_o, t0=t0, fob=fob):
                        sy.wait_ge(b.vE.h, v_o)
                        dv = dst_dram.rearrange("(i p) t -> p i t", p=128)
                        sy.dma_start(out=dv[:, ds, t0:t0 + TT],
                                     in_=obv[:, obp, :]).then_inc(fob.h, 16)
                    fob.inc(16)
                    ob_free[obp] = fob.n

                ln_next = None
                for ds in range(NDC):
                    par = ds % 3
                    sl = ds % 2
                    wp_dl = wp_dls[ds]

                    @blk.tensor
                    def _(t, ds=ds, sl=sl, par=par, wp_dl=wp_dl,
                          yf=max(y_free[par], v_h_all)):
                        t.wait_ge(wp_dl[0][0], wp_dl[0][1])
                        t.wait_ge(wp_dl[1][0], wp_dl[1][1])
                        t.wait_ge(b.vE.h, yf)
                        for j in range(64):
                            _li = t.matmul(ps_y[par][:, 0:TT], wpv[:, sl, j, :], hv[:, j, :],
                                     start=(j == 0), stop=(j == 63))
                        _li.then_inc(b.tE.h, 1)
                    wp_free[sl] = b.tE.inc()
                    t_ys[ds] = wp_free[sl]
                    emit_wp_load(ds + 2)

                    if ds == 0:
                        for k4 in range(4):
                            emit_ss(k4)
                            emit_h2(k4 + 2)
                    elif ds == 1:
                        for k4 in range(4, 9):
                            emit_ss(k4)
                            emit_h2(k4 + 2)
                    elif ds == 2:
                        for k4 in range(9, 16):
                            emit_ss(k4)
                            emit_h2(k4 + 2)
                        v_rsb = emit_rs()
                        for dd in range(3):
                            emit_epi(dd, v_rsb)
                    else:
                        emit_epi(ds, v_rsb)

                    if ti + 1 < ntiles:
                        if ds == 5:
                            fxt = b.ds(pref + "xt")

                            @blk.sync
                            def _(sy, t0n=t0 + TT, fxt=fxt, v_ln=v_ln):
                                sy.wait_ge(b.vE.h, v_ln)
                                sy.dma_start(
                                    out=xtile[:, :].rearrange("p (i t) -> p i t", t=TT),
                                    in_=src_dram.rearrange("(i p) t -> p i t", p=128)[:, :, t0n:t0n + TT]
                                    ).then_inc(fxt.h, 16)
                            fxt.inc(16)
                        elif ds == 6:
                            setup_ln_bufs(ps_, f"{ti}n", h2buf=(h2buf, mss))
                            ln_stages, ln_out = emit_ln_stages(
                                b, C, xsrc, lambda i: xnv[:, i, :], TT, b.bar())
                            ln_stages[0]()
                        elif ds == 8:
                            ln_stages[1]()
                        elif ds == 9:
                            ln_stages[2]()
                        elif ds == 10:
                            ln_stages[3]()
                        elif ds == 11:
                            ln_stages[4]()
                            ln_carry[ti + 1] = dict(v_ln=ln_out["v_ln"],
                                                    t_stats=ln_out["t_stats"])



def emit_attn(b, d, C):
    blk = b.blk
    nc = b.nc
    with ExitStack() as as_:
        e = as_.enter_context
        kF = e(nc.sbuf_tensor("a_kF", [128, NKV * TLOC], BF16))
        vsb = e(nc.sbuf_tensor("a_v", [128, 12 * 512], BF16))
        oF = e(nc.sbuf_tensor("a_oF", [128, NH * TOWN], BF16))
        sb_rope = e(nc.sbuf_tensor("a_rope", [128, 2 * 12 * 64], F32))
        C["cosv"] = sb_rope[:, 0:768].rearrange("p (ts i) -> p ts i", i=64)
        C["sinv"] = sb_rope[:, 768:1536].rearrange("p (ts i) -> p ts i", i=64)
        xn2own = e(nc.sbuf_tensor("a_xn2", [128, NDC * TOWN], BF16))

        kFv = kF[:, :].rearrange("p (h j) -> p h j", j=TLOC)
        vv = vsb[:, :].rearrange("p (ts e) -> p ts e", e=512)
        oFv = oF[:, :].rearrange("p (h a) -> p h a", a=TOWN)
        xn2v = xn2own[:, :].rearrange("p (i t) -> p i t", t=TOWN)

        rope_bar = b.bar()

        rps = b.ds("rope")

        @blk.sync
        def _(sy):
            sy.dma_start(out=C["cosv"],
                         in_=d["rope_cos"].rearrange("(ts p) i -> p ts i", p=128)).then_inc(rps.h, 16)
            rps.inc(16)
            sy.dma_start(out=C["sinv"],
                         in_=d["rope_sin"].rearrange("(ts p) i -> p ts i", p=128)).then_inc(rps.h, 16)
            rps.inc(16)
        rope_dl = (rps.h, rps.n)

        with ExitStack() as s1:
            e1 = s1.enter_context
            wkT = e1(nc.sbuf_tensor("a_wkT", [128, 16 * 512], BF16))
            wvT = e1(nc.sbuf_tensor("a_wvT", [128, 16 * 512], BF16))
            wkTv = wkT[:, :].rearrange("p (i f) -> p i f", f=512)
            wvTv = wvT[:, :].rearrange("p (i f) -> p i f", f=512)

            with ExitStack() as pw:
                ew = pw.enter_context
                wnat = ew(nc.sbuf_tensor("a_wnat", [128, 2 * 2048], F32))
                ps_W = [ew(nc.psum_tensor(f"a_ps_W{k}", [128, 2048], F32))
                        for k in range(2)]
                wnatv = wnat[:, :].rearrange("p (r f) -> p r f", r=2)
                psW_free = [0, 0]
                wnat_free = [0, 0]

                units = [(nm, dstv, r) for nm, dstv in (("wk", wkTv), ("wv", wvTv))
                         for r in range(4)]
                n_dls = {}

                def emit_wl(u):
                    if u >= len(units):
                        return
                    nm, dstv, r = units[u]
                    sl = u % 2
                    wns = b.ds(f"wnat{sl}")

                    @blk.sync
                    def _(sy, nm=nm, r=r, sl=sl, u=u, wns=wns, wf=wnat_free[sl]):
                        if wf > 0:
                            sy.wait_ge(b.tE.h, wf)
                        if u == 0:
                            b.wait_bar(sy, b.bar())
                        sy.dma_start(out=wnatv[:, sl, :],
                                     in_=d[nm].rearrange("(r p) dd -> r p dd",
                                                         p=128)[r]
                                     ).then_inc(wns.h, 16)
                    wns.inc(16)
                    n_dls[u] = (wns.h, wns.n)

                emit_wl(0)
                emit_wl(1)
                for u in range(len(units)):
                    nm, dstv, r = units[u]
                    sl = u % 2

                    @blk.tensor
                    def _(t, sl=sl, n_dl=n_dls[u], pf=psW_free[sl]):
                        t.wait_ge(n_dl[0], n_dl[1])
                        if pf > 0:
                            t.wait_ge(b.vE.h, pf)
                        for i in range(16):
                            _li = t.transpose(ps_W[sl][:, i * 128:(i + 1) * 128],
                                        wnatv[:, sl, i * 128:(i + 1) * 128],
                                        C["idnf"][:, :])
                        _li.then_inc(b.tE.h, 1)
                    t_tr = b.tE.inc()
                    wnat_free[sl] = t_tr
                    emit_wl(u + 2)

                    @blk.vector
                    def _(v, r=r, sl=sl, t_tr=t_tr, dstv=dstv):
                        v.wait_ge(b.tE.h, t_tr)
                        v.tensor_copy(dstv[:, :, r * 128:(r + 1) * 128],
                                      ps_W[sl][:, :].rearrange("p (i f) -> p i f", f=128)
                                      ).then_inc(b.vE.h, 1)
                    psW_free[sl] = b.vE.inc()

            with ExitStack() as s1b:
                eb = s1b.enter_context
                xmt = eb(nc.sbuf_tensor("a_xmt", [128, NDC * TT], F32))
                xmtv = xmt[:, :].rearrange("p (i t) -> p i t", t=TT)
                xnst = eb(nc.sbuf_tensor("a_xnst", [128, NDC * TT], BF16))
                xnstv = xnst[:, :].rearrange("p (i t) -> p i t", t=TT)
                xn_st_free = [0]

                for ti in range(3):
                    t0 = ti * TT
                    bar = b.bar()
                    xms = b.ds("xmt")

                    @blk.sync
                    def _(sy, t0=t0, bar=bar, ti=ti, xms=xms):
                        sy.wait_ge(b.vE.h, bar["v"])
                        if ti == 0:
                            b.wait_bar(sy, bar)
                        sy.dma_start(out=xmtv[:, :, :],
                                     in_=d["xmid"].rearrange("(i p) t -> p i t", p=128)[:, :, t0:t0 + TT]
                                     ).then_inc(xms.h, 16)
                    xms.inc(16)
                    x_dl = (xms.h, xms.n)

                    @blk.vector
                    def _(v, x_dl=x_dl, ti=ti, xf=xn_st_free[0]):
                        v.wait_ge(x_dl[0], x_dl[1])
                        if ti == 0:
                            v.wait_ge(rope_dl[0], rope_dl[1])
                        if xf > 0:
                            v.wait_ge(b.ds("xn2").h, xf)
                        v.engine_nop().then_inc(b.vE.h, 1)
                    b.vE.inc()

                    def xsrc(i):
                        return xmtv[:, i, :]

                    def xn_dst(i):
                        return xnstv[:, i, :]

                    TH = TT // 2
                    with ExitStack() as lns:
                        el = lns.enter_context
                        sts = []
                        outs = []
                        for hf in range(2):
                            C["ln_xbf"] = oF[:, hf * 2 * NDC * TH:(hf + 1) * 2 * NDC * TH]
                            C["ln_ubuf"] = el(nc.sbuf_tensor(f"a_lu{ti}{hf}", [128, TH], F32))
                            C["ln_msb"] = el(nc.sbuf_tensor(f"a_lm{ti}{hf}", [1, TH], F32))
                            C["ln_m2"] = el(nc.sbuf_tensor(f"a_l2{ti}{hf}", [1, TH], F32))
                            C["ln_e2"] = el(nc.sbuf_tensor(f"a_le{ti}{hf}", [1, TH], F32))
                            C["ln_rssb"] = el(nc.sbuf_tensor(f"a_lr{ti}{hf}", [1, TH], F32))
                            C["ln_ps_sx"] = el(nc.psum_tensor(f"a_lpx{ti}{hf}", [128, TT], F32))[:, 0:TH]
                            C["ln_ps_sxx"] = el(nc.psum_tensor(f"a_lpy{ti}{hf}", [128, TT], F32))[:, 0:TH]
                            C["ln_ps_M"] = el(nc.psum_tensor(f"a_lpm{ti}{hf}", [128, TT], F32))[:, 0:TH]
                            C["ln_ps_RS"] = el(nc.psum_tensor(f"a_lpr{ti}{hf}", [128, TT], F32))[:, 0:TH]
                            st, out = emit_ln_stages(
                                b, C,
                                (lambda i, hf=hf: xsrc(i)[:, hf * TH:(hf + 1) * TH]),
                                (lambda i, hf=hf: xn_dst(i)[:, hf * TH:(hf + 1) * TH]),
                                TH, b.bar())
                            sts.append(st)
                            outs.append(out)
                        for stg in range(5):
                            sts[0][stg]()
                            sts[1][stg]()
                        v_ln = outs[1]["v_ln"]

                    xns = b.ds("xn2")

                    @blk.sync
                    def _(sy, t0=t0, v_ln=v_ln, xns=xns):
                        sy.wait_ge(b.vE.h, v_ln)
                        sy.dma_start(out=d["xn2"].rearrange("(i p) t -> p i t", p=128)[:, :, t0:t0 + TT],
                                     in_=xnstv[:, :, :]).then_inc(xns.h, 16)
                    xns.inc(16)
                    xn_st_free[0] = xns.n
            xn2_done = b.ds("xn2").n

            with ExitStack() as s1c:
                ec_ = s1c.enter_context
                xns_b = ec_(nc.sbuf_tensor("a_xnsb", [128, 2 * 16 * 128], BF16))
                xnsv = xns_b[:, :].rearrange("p (k i f) -> p k i f", k=2, i=16)
                ktm = ec_(nc.sbuf_tensor("a_ktm", [128, 12 * 512], BF16))
                ktmv = ktm[:, :].rearrange("p (ts f) -> p ts f", f=512)
                kro = ec_(nc.sbuf_tensor("a_kro", [128, 2 * 512], BF16))
                krov = kro[:, :].rearrange("p (k f) -> p k f", k=2)
                rsc = ec_(nc.sbuf_tensor("a_rsc", [128, 512], F32))
                ps_kv = [ec_(nc.psum_tensor(f"a_pkv{k}", [128, 512], F32)) for k in range(4)]
                ps_T2 = [ec_(nc.psum_tensor(f"a_psT2{k}", [128, 512], BF16)) for k in range(2)]

                kv_free = [0, 0, 0, 0]
                xn_cons = [0, 0]
                kro_cons = [0, 0]
                psT2_free = [0, 0]
                v_kros = {}

                def emit_rope(ts):
                    pp = ts % 2

                    @blk.vector
                    def _(v, ts=ts, pp=pp, kc=kro_cons[pp], kev=kv_free[ts % 2]):
                        v.wait_ge(b.tE.h, kc)
                        v.wait_ge(b.vE.h, kev)
                        kt = ktmv[:, ts, :].rearrange("p (h i two) -> p h i two", h=4, two=2)
                        kr = krov[:, pp, :].rearrange("p (h i two) -> p h i two", h=4, two=2)
                        cc = C["cosv"][:, ts, :].unsqueeze(1).unsqueeze(3).broadcast_to((128, 4, 64, 1))
                        ss = C["sinv"][:, ts, :].unsqueeze(1).unsqueeze(3).broadcast_to((128, 4, 64, 1))
                        r0 = rsc[:, 0:256].rearrange("p (h i) -> p h i", h=4).unsqueeze(3)
                        r1 = rsc[:, 256:512].rearrange("p (h i) -> p h i", h=4).unsqueeze(3)
                        v.tensor_tensor(r0, kt[:, :, :, 0:1], cc, op=ALU.mult)
                        li = v.tensor_tensor(r1, kt[:, :, :, 1:2], ss, op=ALU.mult)
                        b.edge(v, li, "vE")
                        li = v.tensor_tensor(kr[:, :, :, 0:1], r0, r1, op=ALU.subtract)
                        b.edge(v, li, "vE")
                        v.tensor_tensor(r0, kt[:, :, :, 0:1], ss, op=ALU.mult)
                        li = v.tensor_tensor(r1, kt[:, :, :, 1:2], cc, op=ALU.mult)
                        b.edge(v, li, "vE")
                        v.tensor_tensor(kr[:, :, :, 1:2], r0, r1, op=ALU.add).then_inc(b.vE.h, 1)
                    v_kros[ts] = b.vE.inc(1)

                def emit_trev(ts):
                    pp = ts % 2

                    @blk.tensor
                    def _(t, ts=ts, pp=pp, v_kro=v_kros[ts], ptf=psT2_free[pp]):
                        t.wait_ge(b.vE.h, max(v_kro, ptf))
                        for ecc in range(4):
                            _li = t.transpose(ps_T2[pp][:, ecc * 128:(ecc + 1) * 128],
                                        krov[:, pp, ecc * 128:(ecc + 1) * 128],
                                        C["idn"][:, :])
                        _li.then_inc(b.tE.h, 1)
                    t_ktr = b.tE.inc()
                    kro_cons[pp] = t_ktr

                    @blk.vector
                    def _(v, ts=ts, pp=pp, t_ktr=t_ktr):
                        v.wait_ge(b.tE.h, t_ktr)
                        v.tensor_copy(kFv[:, :, ts * 128:(ts + 1) * 128],
                                      ps_T2[pp][:, :].rearrange("p (h f) -> p h f", f=128)
                                      ).then_inc(b.vE.h, 1)
                    psT2_free[pp] = b.vE.inc()

                for ts in range(12):
                    sl = ts % 2
                    xns2 = b.ds(f"xns{sl}")

                    @blk.sync
                    def _(sy, ts=ts, sl=sl, xns2=xns2, xc=xn_cons[sl], xd=xn2_done):
                        sy.wait_ge(b.ds("xn2").h, xd)
                        if xc > 0:
                            sy.wait_ge(b.tE.h, xc)
                        sy.dma_start(out=xnsv[:, sl, :, :],
                                     in_=d["xn2"].rearrange("(i p) t -> p i t", p=128)[:, :, ts * 128:(ts + 1) * 128]
                                     ).then_inc(xns2.h, 16)
                    xns2.inc(16)
                    xn_dl = (xns2.h, xns2.n)

                    for kv in range(2):
                        wTv_ = wkTv if kv == 0 else wvTv
                        bank = 2 * kv + sl

                        @blk.tensor
                        def _(t, ts=ts, kv=kv, sl=sl, bank=bank, wTv_=wTv_, xn_dl=xn_dl):
                            t.wait_ge(xn_dl[0], xn_dl[1])
                            t.wait_ge(b.vE.h, kv_free[bank])
                            for i in range(NDC):
                                _li = t.matmul(ps_kv[bank][:, :],
                                         xnsv[:, sl, i, :],
                                         wTv_[:, i, :], start=(i == 0), stop=(i == NDC - 1))
                            _li.then_inc(b.tE.h, 1)
                        t_kv = b.tE.inc()
                        if kv == 1:
                            xn_cons[sl] = t_kv

                        @blk.vector
                        def _(v, ts=ts, kv=kv, bank=bank, t_kv=t_kv):
                            v.wait_ge(b.tE.h, t_kv)
                            if kv == 0:
                                v.tensor_copy(ktmv[:, ts, :], ps_kv[bank][:, :]).then_inc(b.vE.h, 1)
                            else:
                                v.tensor_copy(vv[:, ts, :], ps_kv[bank][:, :]).then_inc(b.vE.h, 1)
                        kv_free[bank] = b.vE.inc()

                    emit_rope(ts)
                    if ts >= 1:
                        emit_trev(ts - 1)
                emit_trev(11)

        with ExitStack() as s2:
            e2_ = s2.enter_context
            qF = e2_(nc.sbuf_tensor("a_qF", [128, NH * TOWN], BF16))
            qFv = qF[:, :].rearrange("p (h a) -> p h a", a=TOWN)
            s2i = e2_(ExitStack())
            e2b = s2i.enter_context
            qtm = e2b(nc.sbuf_tensor("a_qtm", [128, 8 * 2048], BF16))
            qtmv = qtm[:, :].rearrange("p (ts e) -> p ts e", e=2048)
            wqT = e2b(nc.sbuf_tensor("a_wqT", [128, 16 * 512], BF16))
            wqTv = wqT[:, :].rearrange("p (i f) -> p i f", f=512)
            wnat2 = e2b(nc.sbuf_tensor("a_wnat2", [128, 2 * 2048], F32))
            wnat2v = wnat2[:, :].rearrange("p (r f) -> p r f", r=2)
            xns_b2 = e2b(nc.sbuf_tensor("a_xnsb2", [128, 2 * 16 * 128], BF16))
            xnsv2 = xns_b2[:, :].rearrange("p (k i f) -> p k i f", k=2, i=16)
            qro = oF[:, 0:2048]
            rsc2 = oF[:, 2048:4096]
            ps_q = [e2b(nc.psum_tensor(f"a_pq{k}", [128, 512], F32)) for k in range(2)]
            ps_W2 = [e2b(nc.psum_tensor(f"a_psW2{k}", [128, 1024], F32))
                     for k in range(2)]
            ps_T3 = e2b(nc.psum_tensor("a_psT3", [128, 2048], BF16))

            psW2_free = [0, 0]
            wnat2_free = [0, 0]
            q_free = [0, 0]
            qro_cons = [0]
            psT3_free = [0]
            xn_cons2 = [0, 0]
            a2_bar = b.bar()
            wq_dls = {}

            def emit_wql(g):
                if g >= 16:
                    return
                sl = g % 2
                wns2 = b.ds(f"wnat2{sl}")

                @blk.sync
                def _(sy, g=g, sl=sl, wns2=wns2, wf=wnat2_free[sl]):
                    if g == 0:
                        b.wait_bar(sy, a2_bar)
                    if wf > 0:
                        sy.wait_ge(b.tE.h, wf)
                    sy.dma_start(out=wnat2v[:, sl, :],
                                 in_=d["wq"].rearrange("(g p) dd -> g p dd",
                                                       p=128)[g]
                                 ).then_inc(wns2.h, 16)
                wns2.inc(16)
                wq_dls[g] = (wns2.h, wns2.n)

            emit_wql(0)
            emit_wql(1)
            for ec in range(4):
                for r in range(4):
                    g = 4 * ec + r
                    sl = g % 2
                    for hh in range(2):
                        q2 = (2 * g + hh) % 2

                        @blk.tensor
                        def _(t, sl=sl, hh=hh, q2=q2, n_dl=wq_dls[g],
                              pf=psW2_free[q2]):
                            t.wait_ge(n_dl[0], n_dl[1])
                            if pf > 0:
                                t.wait_ge(b.vE.h, pf)
                            for i in range(8):
                                _li = t.transpose(ps_W2[q2][:, i * 128:(i + 1) * 128],
                                            wnat2v[:, sl, hh * 1024 + i * 128:hh * 1024 + (i + 1) * 128],
                                            C["idnf"][:, :])
                            _li.then_inc(b.tE.h, 1)
                        t_tr = b.tE.inc()
                        if hh == 1:
                            wnat2_free[sl] = t_tr
                            emit_wql(g + 2)

                        @blk.vector
                        def _(v, r=r, hh=hh, q2=q2, t_tr=t_tr):
                            v.wait_ge(b.tE.h, t_tr)
                            v.tensor_copy(wqTv[:, 8 * hh:8 * (hh + 1), r * 128:(r + 1) * 128],
                                          ps_W2[q2][:, :].rearrange("p (i f) -> p i f", f=128)
                                          ).then_inc(b.vE.h, 1)
                        psW2_free[q2] = b.vE.inc()
                v_wq = b.vE.n

                for ts in range(8):
                    pp = ts % 2
                    if ec == 0:
                        xnq = b.ds(f"xnq{pp}")

                        @blk.sync
                        def _(sy, ts=ts, pp=pp, xnq=xnq, xc=xn_cons2[pp]):
                            if xc > 0:
                                sy.wait_ge(b.tE.h, xc)
                            sy.dma_start(out=xnsv2[:, pp, :, :],
                                         in_=d["xn2"].rearrange("(i p) t -> p i t", p=128)[:, :, HALO + ts * 128: HALO + (ts + 1) * 128]
                                         ).then_inc(xnq.h, 16)
                        xnq.inc(16)
                    if ec > 0:
                        xnq = b.ds(f"xnq{pp}")

                        @blk.sync
                        def _(sy, ts=ts, pp=pp, xnq=xnq, xc=xn_cons2[pp]):
                            sy.wait_ge(b.tE.h, xc)
                            sy.dma_start(out=xnsv2[:, pp, :, :],
                                         in_=d["xn2"].rearrange("(i p) t -> p i t", p=128)[:, :, HALO + ts * 128: HALO + (ts + 1) * 128]
                                         ).then_inc(xnq.h, 16)
                        xnq.inc(16)
                    xq_dl = (b.ds(f"xnq{pp}").h, b.ds(f"xnq{pp}").n)

                    @blk.tensor
                    def _(t, ec=ec, ts=ts, pp=pp, v_wq=v_wq, xq_dl=xq_dl,
                          qf=q_free[pp]):
                        t.wait_ge(xq_dl[0], xq_dl[1])
                        t.wait_ge(b.vE.h, max(v_wq, qf))
                        for i in range(NDC):
                            _li = t.matmul(ps_q[pp][:, :], xnsv2[:, pp, i, :],
                                     wqTv[:, i, :], start=(i == 0), stop=(i == NDC - 1))
                        _li.then_inc(b.tE.h, 1)
                    t_q = b.tE.inc()
                    xn_cons2[pp] = t_q

                    @blk.vector
                    def _(v, ec=ec, ts=ts, pp=pp, t_q=t_q):
                        v.wait_ge(b.tE.h, t_q)
                        v.tensor_copy(qtmv[:, ts, ec * 512:(ec + 1) * 512],
                                      ps_q[pp][:, :]).then_inc(b.vE.h, 1)
                    q_free[pp] = b.vE.inc()

            v_qtm = b.vE.n
            for ts in range(8):
                @blk.vector
                def _(v, ts=ts, v_qtm=v_qtm):
                    v.wait_ge(b.vE.h, v_qtm)
                    v.wait_ge(b.tE.h, qro_cons[0])
                    gts = 4 + ts
                    qt = qtmv[:, ts, :].rearrange("p (h i two) -> p h i two", h=16, two=2)
                    qr = qro.rearrange("p (h i two) -> p h i two", h=16, two=2)
                    cc = C["cosv"][:, gts, :].unsqueeze(1).unsqueeze(3).broadcast_to((128, 16, 64, 1))
                    ss = C["sinv"][:, gts, :].unsqueeze(1).unsqueeze(3).broadcast_to((128, 16, 64, 1))
                    r0 = rsc2[:, 0:1024].rearrange("p (h i) -> p h i", h=16).unsqueeze(3)
                    r1 = rsc2[:, 1024:2048].rearrange("p (h i) -> p h i", h=16).unsqueeze(3)
                    v.tensor_tensor(r0, qt[:, :, :, 0:1], cc, op=ALU.mult)
                    li = v.tensor_tensor(r1, qt[:, :, :, 1:2], ss, op=ALU.mult)
                    b.edge(v, li, "vE")
                    li = v.tensor_tensor(qr[:, :, :, 0:1], r0, r1, op=ALU.subtract)
                    b.edge(v, li, "vE")
                    v.tensor_tensor(r0, qt[:, :, :, 0:1], ss, op=ALU.mult)
                    li = v.tensor_tensor(r1, qt[:, :, :, 1:2], cc, op=ALU.mult)
                    b.edge(v, li, "vE")
                    v.tensor_tensor(qr[:, :, :, 1:2], r0, r1, op=ALU.add).then_inc(b.vE.h, 1)
                v_qro = b.vE.inc(1)

                @blk.tensor
                def _(t, ts=ts, v_qro=v_qro):
                    t.wait_ge(b.vE.h, max(v_qro, psT3_free[0]))
                    for hc in range(16):
                        _li = t.transpose(ps_T3[:, hc * 128:(hc + 1) * 128],
                                    qro[:, hc * 128:(hc + 1) * 128],
                                    C["idn"][:, :])
                    _li.then_inc(b.tE.h, 1)
                qro_cons[0] = b.tE.inc()
                t_qtr = qro_cons[0]

                @blk.vector
                def _(v, ts=ts, t_qtr=t_qtr):
                    v.wait_ge(b.tE.h, t_qtr)
                    v.tensor_copy(qFv[:, :, ts * 128:(ts + 1) * 128],
                                  ps_T3[:, :].rearrange("p (h f) -> p h f", f=128)
                                  ).then_inc(b.vE.h, 1)
                psT3_free[0] = b.vE.inc()

            s2i.close()
            with ExitStack() as s3:
                e3 = s3.enter_context
                tb_t = e3(nc.sbuf_tensor("a_tb", [128, 2 * 512], F32))
                tb = tb_t[:, :].rearrange("p (k f) -> p k f", k=2)
                pT = e3(nc.sbuf_tensor("a_pT", [128, 8 * 512], BF16))
                dsm = e3(nc.sbuf_tensor("a_dsm", [1, 512], F32))
                rbb = e3(nc.sbuf_tensor("a_rbb", [128, 512], F32))
                ps_sT = [e3(nc.psum_tensor(f"a_pst{k}", [128, 512], F32)) for k in range(2)]
                ps_den = e3(nc.psum_tensor("a_psd", [128, 512], F32))
                ps_rb = e3(nc.psum_tensor("a_psrb", [128, 512], F32))
                ps_o = [e3(nc.psum_tensor(f"a_pso{k}", [128, 512], F32)) for k in range(2)]

                sb_masks = e3(nc.sbuf_tensor("a_masks", [128, 2 * 8 * 512], BF16))
                maskTv = sb_masks[:, :].rearrange("p (bq kt j) -> p bq kt j", bq=2, kt=8)

                mks = b.ds("mask")
                a3_bar = b.bar()

                @blk.sync
                def _(sy, mks=mks):
                    b.wait_bar(sy, a3_bar)
                    sy.dma_start(out=maskTv,
                                 in_=d["masks"].rearrange("bq m p j -> p bq m j")
                                 ).then_inc(mks.h, 16)
                    mks.inc(16)
                mask_dl = (mks.h, mks.n)

                pTv = pT[:, :].rearrange("p (j a) -> p j a", a=512)
                v_attn_in = b.vE.n

                heads = [(bq, h) for bq in range(2) for h in range(NH)]
                NHD = len(heads)
                sT_rec = {}
                pv_rec = {}
                den_rec = {}
                rb_rec = {}

                def s_h0f(hi):
                    return sA3 + 16 * hi

                def v_h0f(hi):
                    return vA3 + 11 * hi

                def emit_sT(hi, kt):
                    if hi >= NHD:
                        return
                    bq, h = heads[hi]
                    kvh = h // 4

                    @blk.tensor
                    def _(t, bq=bq, h=h, kt=kt, kvh=kvh, hi=hi):
                        if hi == 0 and kt == 0:
                            t.wait_ge(b.vE.h, v_attn_in)
                        if kt >= 2:
                            t.wait_ge(b.sE.h, s_h0f(hi) + 2 * (kt - 2) + 1)
                        elif hi > 0:
                            t.wait_ge(b.sE.h, s_h0f(hi - 1) + 2 * (kt + 6) + 1)
                        t.matmul(ps_sT[kt % 2][:, :],
                                 kFv[:, kvh, bq * 512 + kt * 128: bq * 512 + (kt + 1) * 128],
                                 qFv[:, h, bq * 512: (bq + 1) * 512],
                                 start=True, stop=True).then_inc(b.tE.h, 1)
                    sT_rec[(hi, kt)] = b.tE.inc()

                sA3 = b.sE.n
                vA3 = b.vE.n
                emit_sT(0, 0)
                emit_sT(0, 1)

                for hi in range(NHD):
                    bq, h = heads[hi]
                    kvh = h // 4
                    s_h0 = b.sE.n
                    v_h0 = b.vE.n
                    assert s_h0 == s_h0f(hi) and v_h0 == v_h0f(hi)

                    for kt in range(8):
                        if kt < 6:
                            emit_sT(hi, kt + 2)
                        else:
                            emit_sT(hi + 1, kt - 6)

                        @blk.tensor
                        def _(t, kt=kt, v_h0=v_h0):
                            t.wait_ge(b.vE.h, v_h0 + kt + 1)
                            t.matmul(ps_den[0:1, :], C["onesb"][:, :], pTv[:, kt, :],
                                     start=(kt == 0), stop=(kt == 7)).then_inc(b.tE.h, 1)
                        den_rec[(hi, kt)] = b.tE.inc()

                        @blk.tensor
                        def _(t, bq=bq, kt=kt, kvh=kvh, v_h0=v_h0, op=hi % 2):
                            t.wait_ge(b.vE.h, v_h0 + kt + 1)
                            t.matmul(ps_o[op][:, :], vv[:, bq * 4 + kt, kvh * 128:(kvh + 1) * 128],
                                     pTv[:, kt, :], start=(kt == 0),
                                     stop=(kt == 7)).then_inc(b.tE.h, 1)
                        pv_rec[(hi, kt)] = b.tE.inc()

                    @blk.tensor
                    def _(t, v_h0=v_h0):
                        t.wait_ge(b.vE.h, v_h0 + 9)
                        t.matmul(ps_rb[:, :], C["onesf"][0:1, :], dsm[0:1, :],
                                 start=True, stop=True).then_inc(b.tE.h, 1)
                    rb_rec[hi] = b.tE.inc()

                    @blk.scalar
                    def _(s, hi=hi, s_h0=s_h0):
                        for kt in range(8):
                            s.wait_ge(b.tE.h, sT_rec[(hi, kt)])
                            li = s.activation(tb[:, kt % 2, :], ps_sT[kt % 2][:, :],
                                              AF.Tanh, scale=SQ_SCALE)
                            li.then_inc(b.sE.h, 1)
                            s.wait_ge(b.sE.h, s_h0 + 2 * kt + 1)
                            s.activation(pTv[:, kt, :], tb[:, kt % 2, :], AF.Exp,
                                         scale=CAP).then_inc(b.sE.h, 1)
                    b.sE.inc(16)

                    for kt in range(8):
                        @blk.vector
                        def _(v, bq=bq, kt=kt, hi=hi, s_h0=s_h0):
                            v.wait_ge(b.sE.h, s_h0 + 2 * kt + 2)
                            if hi == 0 and kt == 0:
                                v.wait_ge(mask_dl[0], mask_dl[1])
                            v.tensor_mul(pTv[:, kt, :], pTv[:, kt, :],
                                         maskTv[:, bq, kt, :]).then_inc(b.vE.h, 1)
                        b.vE.inc()

                    @blk.vector
                    def _(v, hi=hi):
                        v.wait_ge(b.tE.h, den_rec[(hi, 7)])
                        v.reciprocal(dsm[0:1, :], ps_den[0:1, :]).then_inc(b.vE.h, 1)
                    b.vE.inc()

                    @blk.vector
                    def _(v, hi=hi):
                        v.wait_ge(b.tE.h, rb_rec[hi])
                        v.tensor_copy(rbb[:, :], ps_rb[:, :]).then_inc(b.vE.h, 1)
                    b.vE.inc()

                    @blk.vector
                    def _(v, bq=bq, h=h, hi=hi, v_h0=v_h0, op=hi % 2):
                        v.wait_ge(b.tE.h, pv_rec[(hi, 7)])
                        v.wait_ge(b.vE.h, v_h0 + 10)
                        v.tensor_mul(oFv[:, h, bq * 512:(bq + 1) * 512],
                                     ps_o[op][:, :], rbb[:, :]).then_inc(b.vE.h, 1)
                    b.vE.inc()
                    assert b.vE.n == v_h0 + 11

        with ExitStack() as s4:
            e4 = s4.enter_context
            woT = e4(nc.sbuf_tensor("a_woT", [128, 16 * 128], BF16))
            woTv = woT[:, :].rearrange("p (j f) -> p j f", f=128)
            wnat3 = e4(nc.sbuf_tensor("a_wnat3", [128, 2048], F32))
            xm2 = e4(nc.sbuf_tensor("a_xm2", [128, 1024], F32))
            stg = e4(nc.sbuf_tensor("a_stg", [128, 2 * 512], F32))
            stgv = stg[:, :].rearrange("p (k t) -> p k t", k=2)
            ps_W3 = e4(nc.psum_tensor("a_psW3", [128, 2048], F32))
            ps_y2 = [e4(nc.psum_tensor(f"a_py2{k}", [128, 512], F32)) for k in range(2)]

            psW3_free = [0]
            wnat3_free = [0]
            y2_free = [0, 0]
            stg_free = [0, 0]

            a4_bar = b.bar()
            for ds in range(NDC):
                bar = b.bar()

                wns3 = b.ds("wnat")

                @blk.sync
                def _(sy, ds=ds, wns3=wns3):
                    if ds == 0:
                        b.wait_bar(sy, a4_bar)
                    sy.wait_ge(b.tE.h, wnat3_free[0])
                    sy.dma_start(out=wnat3[:, :],
                                 in_=d["wo"].rearrange("(ds p) ee -> ds p ee", p=128)[ds]
                                 ).then_inc(wns3.h, 16)
                wns3.inc(16)
                n_dl = (wns3.h, wns3.n)

                xm2s = b.ds("xm2")

                @blk.scalar
                def _(s, ds=ds, bar=bar, xm2s=xm2s):
                    s.wait_ge(b.vE.h, bar["v"])
                    s.dma_start(out=xm2[:, :],
                                in_=d["xmid"].rearrange("(i p) t -> p i t", p=128)[:, ds, HALO:HALO + TOWN]
                                ).then_inc(xm2s.h, 16)
                xm2s.inc(16)
                x_dl = (xm2s.h, xm2s.n)

                @blk.tensor
                def _(t, ds=ds, n_dl=n_dl):
                    t.wait_ge(n_dl[0], n_dl[1])
                    t.wait_ge(b.vE.h, psW3_free[0])
                    for j in range(16):
                        _li = t.transpose(ps_W3[:, j * 128:(j + 1) * 128],
                                    wnat3[:, j * 128:(j + 1) * 128],
                                    C["idnf"][:, :])
                    _li.then_inc(b.tE.h, 1)
                t_tr = b.tE.inc()
                wnat3_free[0] = t_tr

                @blk.vector
                def _(v, t_tr=t_tr):
                    v.wait_ge(b.tE.h, t_tr)
                    v.tensor_copy(woTv[:, :, :],
                                  ps_W3[:, :].rearrange("p (j f) -> p j f", f=128)
                                  ).then_inc(b.vE.h, 1)
                psW3_free[0] = b.vE.inc()
                v_wo = psW3_free[0]

                for ah in range(2):
                    @blk.tensor
                    def _(t, ds=ds, ah=ah, v_wo=v_wo):
                        t.wait_ge(b.vE.h, max(v_wo, y2_free[ah]))
                        for j in range(16):
                            _li = t.matmul(ps_y2[ah][:, :], woTv[:, j, :],
                                     oFv[:, j, ah * 512:(ah + 1) * 512],
                                     start=(j == 0), stop=(j == 15))
                        _li.then_inc(b.tE.h, 1)
                    t_y = b.tE.inc()

                    @blk.vector
                    def _(v, ds=ds, ah=ah, t_y=t_y, x_dl=x_dl):
                        v.wait_ge(b.tE.h, t_y)
                        v.wait_ge(x_dl[0], x_dl[1])
                        if stg_free[ah] > 0:
                            v.wait_ge(b.ds(f"stg{ah}").h, stg_free[ah])
                        v.tensor_add(stgv[:, ah, :], ps_y2[ah][:, :],
                                     xm2[:, ah * 512:(ah + 1) * 512]).then_inc(b.vE.h, 1)
                    y2_free[ah] = b.vE.inc()
                    v_st = y2_free[ah]

                    fst = b.ds(f"stg{ah}")

                    @blk.scalar
                    def _(s, ds=ds, ah=ah, v_st=v_st, fst=fst):
                        s.wait_ge(b.vE.h, v_st)
                        dv = d["xatt"].rearrange("(i p) t -> p i t", p=128)
                        s.dma_start(out=dv[:, ds, ah * 512:(ah + 1) * 512],
                                    in_=stgv[:, ah, :]).then_inc(fst.h, 16)
                    fst.inc(16)
                    stg_free[ah] = fst.n


tb_free = [0]


def build_program():
    global tb_free
    tb_free = [0]
    nc = bass.Bass("TRN2", target_bir_lowering=False)
    st = ExitStack()
    d = {}
    d["x_feat"] = nc.dram_tensor("x_feat", [DM, TLOC], F32, kind="ExternalInput").ap()
    d["wq"] = nc.dram_tensor("wq", [DM, DM], F32, kind="ExternalInput").ap()
    d["wk"] = nc.dram_tensor("wk", [512, DM], F32, kind="ExternalInput").ap()
    d["wv"] = nc.dram_tensor("wv", [512, DM], F32, kind="ExternalInput").ap()
    d["wo"] = nc.dram_tensor("wo", [DM, DM], F32, kind="ExternalInput").ap()
    d["wgu"] = nc.dram_tensor("wgu", [2 * HFF, DM], F32, kind="ExternalInput").ap()
    d["wp"] = nc.dram_tensor("wp", [DM, HFF], F32, kind="ExternalInput").ap()
    d["rope_cos"] = nc.dram_tensor("rope_cos", [TLOC, 64], F32, kind="ExternalInput").ap()
    d["rope_sin"] = nc.dram_tensor("rope_sin", [TLOC, 64], F32, kind="ExternalInput").ap()
    d["masks"] = nc.dram_tensor("masks", [2, 8, 128, 512], BF16, kind="ExternalInput").ap()
    d["gsel"] = nc.dram_tensor("gsel", [16, 128, 32], BF16, kind="ExternalInput").ap()
    d["bsel"] = nc.dram_tensor("bsel", [16, 32, 128], BF16, kind="ExternalInput").ap()
    d["idn_d"] = nc.dram_tensor("idn", [128, 128], BF16, kind="ExternalInput").ap()
    d["idnf_d"] = nc.dram_tensor("idnf", [128, 128], F32, kind="ExternalInput").ap()
    d["onesb_d"] = nc.dram_tensor("onesb", [128, 1], BF16, kind="ExternalInput").ap()
    d["onesf_d"] = nc.dram_tensor("onesf", [1, 128], F32, kind="ExternalInput").ap()
    d["out_feat"] = nc.dram_tensor("out_feat", [DM, TOWN], F32, kind="ExternalOutput").ap()
    d["cache_gu"] = nc.dram_tensor("cache_gu", [64, 128, 16 * 256], BF16).ap()
    d["cache_wp"] = nc.dram_tensor("cache_wp", [16, 128, 64 * 128], BF16).ap()
    d["xmid"] = nc.dram_tensor("xmid", [DM, TLOC], F32).ap()
    d["xatt"] = nc.dram_tensor("xatt", [DM, TOWN], F32).ap()
    d["xn2"] = nc.dram_tensor("xn2", [DM, TLOC], BF16).ap()

    blk = st.enter_context(nc.Block())
    b = B(nc, st, blk)

    sb_gsel = st.enter_context(nc.sbuf_tensor("sb_gsel", [128, 16 * 32], BF16))
    sb_bsel = st.enter_context(nc.sbuf_tensor("sb_bsel", [32, 16 * 128], BF16))
    sb_idn = st.enter_context(nc.sbuf_tensor("sb_idn", [128, 128], BF16))
    sb_idnf = st.enter_context(nc.sbuf_tensor("sb_idnf", [128, 128], F32))
    sb_onesb = st.enter_context(nc.sbuf_tensor("sb_onesb", [128, 1], BF16))
    sb_onesf = st.enter_context(nc.sbuf_tensor("sb_onesf", [1, 128], F32))

    C = dict(
        gselv=sb_gsel[:, :].rearrange("p (i g) -> p i g", g=32),
        bselv=sb_bsel[:, :].rearrange("p (i m) -> p i m", m=128),
        idn=sb_idn, idnf=sb_idnf, onesb=sb_onesb, onesf=sb_onesf,
    )

    @blk.sync
    def _(sy):
        pairs = [
            (sb_gsel[:, :].rearrange("p (i g) -> p i g", g=32),
             d["gsel"].rearrange("i p g -> p i g")),
            (sb_bsel[:, :].rearrange("p (i m) -> p i m", m=128),
             d["bsel"].rearrange("i p m -> p i m")),
            (sb_idn[:, :], d["idn_d"]),
            (sb_idnf[:, :], d["idnf_d"]),
            (sb_onesb[:, :], d["onesb_d"]),
            (sb_onesf[:, :], d["onesf_d"]),
        ]
        cs = b.ds("const")
        for o, i_ in pairs:
            sy.dma_start(out=o, in_=i_).then_inc(cs.h, 16)
            cs.inc(16)
    cs = b.ds("const")

    for eng in ("tensor", "vector", "scalar", "gpsimd"):
        getattr(blk, eng)(lambda e_, h=cs.h, n=cs.n: e_.wait_ge(h, n))

    emit_quant(b, d, C)
    emit_ffn(b, d, C, which=1)
    emit_attn(b, d, C)
    emit_ffn(b, d, C, which=2)

    fin = b.bar()

    @blk.sync
    def _(sy):
        b.wait_bar(sy, fin)

    st.close()
    return nc


def kernel(L0_attn_x, L0_attn_wq, L0_attn_wk, L0_attn_wv, L0_attn_wo,
           L0_ffn_w_gate_up, L0_ffn_w_proj):
    x = np.asarray(L0_attn_x, dtype=np.float32).reshape(S, DM)
    wq = np.ascontiguousarray(np.asarray(L0_attn_wq, dtype=np.float32))
    wk = np.ascontiguousarray(np.asarray(L0_attn_wk, dtype=np.float32))
    wv = np.ascontiguousarray(np.asarray(L0_attn_wv, dtype=np.float32))
    wo = np.ascontiguousarray(np.asarray(L0_attn_wo, dtype=np.float32))
    wgu = np.ascontiguousarray(np.asarray(L0_ffn_w_gate_up, dtype=np.float32))
    wp = np.ascontiguousarray(np.asarray(L0_ffn_w_proj, dtype=np.float32))

    nc = build_program()

    in_maps = []
    for c in range(NCORES):
        consts = _host_consts(c)
        lo = c * TOWN - HALO
        xt = np.zeros((TLOC, DM), dtype=np.float32)
        v0 = max(lo, 0)
        xt[v0 - lo:, :] = x[v0:lo + TLOC, :]
        in_maps.append(dict(x_feat=np.ascontiguousarray(xt.T),
                            wq=wq, wk=wk, wv=wv, wo=wo, wgu=wgu, wp=wp, **consts))

    res = run_bass_kernel_spmd(nc, in_maps, core_ids=list(range(NCORES)))
    out = np.empty((1, S, DM), dtype=np.float32)
    for c in range(NCORES):
        out[0, c * TOWN:(c + 1) * TOWN, :] = np.asarray(res.results[c]["out_feat"]).T
    return out
